# version 08
# speedup vs baseline: 1.2822x; 1.0437x over previous
import math
import numpy as np
import ml_dtypes
import concourse.bass as bass
import concourse.mybir as mybir
from concourse.bass_utils import run_bass_kernel_spmd

F32 = mybir.dt.float32
BF16 = mybir.dt.bfloat16
AF = mybir.ActivationFunctionType
AX = mybir.AxisListType
OP = mybir.AluOpType

ENGINES = ("sync", "tensor", "vector", "scalar", "gpsimd")

D = 1024
DBG = False
NEG_BIG = -30000.0


def cdiv(a, b):
    return (a + b - 1) // b


class Prog:

    PERSIST = ("bar", "done_sync", "done_tensor", "done_vector", "done_scalar",
               "done_gpsimd")
    RECYCLE = True

    def __init__(self, nc):
        self.nc = nc
        self.ops = {e: [] for e in ENGINES}
        self.handles = {}
        self.base = {}
        self.cnt = {}
        self.lastwait = {}
        self.pool = {}
        self.sem_kind = {}
        self.active = set()
        self.act_cnt = {}
        self._hinc_tok = {}
        self._gcanon = {}
        self._gowner = {}
        self._n_phys = 0

    def sem(self, name, kind="norm"):
        if name not in self.active:
            if name in self.PERSIST and name in self.handles:
                self.active.add(name)
                return name
            kind = self.sem_kind.setdefault(name, kind)
            if kind == "swdge":
                idx = sum(1 for a in self.active
                          if self.sem_kind.get(a) == "swdge")
                cname = "GD%d" % (idx % 14)
                if cname not in self._gcanon:
                    self._n_phys += 1
                    self._gcanon[cname] = [self.nc.alloc_semaphore("s%d" % self._n_phys), 0]
                h, v = self._gcanon[cname]
                self.handles[name] = h
                self.base[name] = v
                self.cnt.setdefault(name, 0)
                self.act_cnt[name] = self.cnt[name]
                self.active.add(name)
                self._gowner[name] = cname
                return name
            pool = self.pool.setdefault(kind, [])
            if pool and name not in self.PERSIST:
                h, v = pool.pop()
            else:
                self._n_phys += 1
                h, v = self.nc.alloc_semaphore("s%d" % self._n_phys), 0
            self.handles[name] = h
            self.base[name] = v
            self.cnt.setdefault(name, 0)
            self.act_cnt[name] = self.cnt[name]
            self.active.add(name)
        return name

    def retire_all(self):
        for name in list(self.active):
            if name in self.PERSIST:
                continue
            k = self.sem_kind.get(name, "norm")
            v = self.base[name] + (self.cnt[name] - self.act_cnt[name])
            if k == "swdge":
                self._gcanon[self._gowner[name]][1] = v
            else:
                self.pool.setdefault(k, []).append((self.handles[name], v))
            self.active.discard(name)

    def op(self, eng, fn, deps=(), inc=None, n=1):
        waits = []
        for dd in deps:
            if dd is None:
                continue
            sem, val = dd
            if sem not in self.active:
                continue
            if val <= self.act_cnt.get(sem, 0):
                continue
            key = (eng, sem)
            if val > self.lastwait.get(key, 0):
                waits.append((self.handles[sem],
                              self.base[sem] + (val - self.act_cnt[sem])))
                self.lastwait[key] = val
        tok = None
        hinc = None
        if inc is not None:
            self.sem(inc)
            self.cnt[inc] += n
            tok = (inc, self.cnt[inc])
            hinc = (self.handles[inc], self.cnt[inc] - self.act_cnt[inc])
            self._hinc_tok[id(hinc)] = tok
        self.ops[eng].append([waits, fn, hinc, n])
        return tok

    def wait(self, eng, deps):
        self.op(eng, None, deps=deps)

    def dma(self, eng, out, in_, deps=(), inc=None):
        def fn(E, out=out, in_=in_):
            return E.dma_start(out=out, in_=in_)
        if inc is None:
            inc = "dma_" + eng
        self.sem(inc, kind=("swdge" if eng == "gpsimd" else "hwdge"))
        return self.op(eng, fn, deps=deps, inc=inc, n=16)

    def last_tok(self, eng):
        for rec in reversed(self.ops[eng]):
            if rec[1] is not None:
                if rec[2] is None:
                    name = self.sem("done_" + eng)
                    self.cnt[name] += rec[3]
                    rec[2] = (self.handles[name], self.cnt[name] - self.act_cnt[name])
                    return (name, self.cnt[name])
                return self._tok_of(rec[2])
        return None

    def _tok_of(self, hinc):
        return self._hinc_tok.get(id(hinc))

    def barrier(self):
        bar = self.sem("bar")
        deps = [self.last_tok(e) for e in ENGINES]
        deps += [(s, c) for s, c in self.cnt.items()
                 if s in self.active and c > self.act_cnt.get(s, 0) and s != "bar"]
        self.op("sync", lambda E: E.nop(), deps=deps, inc=bar)
        tgt = self.cnt[bar]
        for e in ENGINES:
            if e != "sync":
                self.wait(e, [(bar, tgt)])
        self.lastwait[("sync", bar)] = tgt
        if self.RECYCLE:
            self.retire_all()

    def emit(self):
        prog = self

        def runner(engine_name):
            def body(E):
                for waits, fn, inc, n in prog.ops[engine_name]:
                    for hh, val in waits:
                        E.wait_ge(hh, val)
                    if fn is not None:
                        inst = fn(E)
                        if inc is not None:
                            inst.then_inc(inc[0], n)
                    elif inc is not None:
                        E.nop().then_inc(inc[0], n)
            return body

        with self.nc.Block() as block:
            block.sync(runner("sync"))
            block.tensor(runner("tensor"))
            block.vector(runner("vector"))
            block.scalar(runner("scalar"))
            block.gpsimd(runner("gpsimd"))


class Builder:

    def __init__(self, T=4096, phases=("p1", "p2", "p3", "pd", "pe1", "pe2", "pe3")):
        self.T = T
        self.NTB = T // 128
        self.phases = phases
        nc = bass.Bass("TRN2", target_bir_lowering=False, debug=False)
        self.nc = nc
        self.P = Prog(nc)
        self.dram = {}
        self.dram_kind = {}
        self.sb = {}
        self._free_list = []
        self.constf_vals = []
        self._cpool = nc.alloc_sbuf_tensor("c_pool", [128, 16], F32)
        self.sb["c_pool"] = self._cpool
        self.psA = nc.alloc_psum_tensor("psA", [128, 1024], F32)
        self.psB = nc.alloc_psum_tensor("psB", [128, 1024], F32)
        self.psC = nc.alloc_psum_tensor("psC", [128, 1024], F32)
        self.psT = nc.alloc_psum_tensor("psT", [128, 2048], BF16)
        self.psD = None

    def din(self, name, shape, dtype=F32):
        if name not in self.dram:
            self.dram[name] = self.nc.dram_tensor(name, list(shape), dtype, kind="ExternalInput").ap()
            self.dram_kind[name] = "in"
        return self.dram[name]

    def dout(self, name, shape, dtype=F32):
        if name not in self.dram:
            self.dram[name] = self.nc.dram_tensor(name, list(shape), dtype, kind="ExternalOutput").ap()
            self.dram_kind[name] = "out"
        return self.dram[name]

    force_out = ()

    def dscratch(self, name, shape, dtype=F32):
        if name not in self.dram:
            kind = "ExternalOutput" if name in self.force_out else "Internal"
            self.dram[name] = self.nc.dram_tensor(name, list(shape), dtype, kind=kind).ap()
            self.dram_kind[name] = "out" if name in self.force_out else "scratch"
        return self.dram[name]

    def sbuf(self, name, shape, dtype):
        h = self.nc.alloc_sbuf_tensor(name, list(shape), dtype)
        self.sb[name] = h
        return h

    def constf(self, val):
        val = float(val)
        if val not in self.constf_vals:
            assert len(self.constf_vals) < 16
            col = len(self.constf_vals)
            self.constf_vals.append(val)
            self.nc.const_aps.register(F32, val, self._cpool.ap()[:, col:col + 1])
        return val

    def phase_bufs(self, stack, specs):
        self._bufgen = getattr(self, "_bufgen", 0) + 1
        for name, (shape, dtype) in specs.items():
            g = self.nc.sbuf_tensor("%s__%d" % (name, self._bufgen), list(shape), dtype)
            h = stack.enter_context(g)
            self.sb[name] = h
        return self.sb


def norm_stats(B, xt_ap, st, eps, dep, kind="ln"):
    P = B.P
    F = xt_ap.shape[1]
    for v in (eps, 1.0 / F, -0.5, -1.0, 0.5):
        B.constf(v)
    negmean = st[:, 0:1]
    ssq = st[:, 1:2]
    tmp = st[:, 2:3]
    rstd = st[:, 3:4]
    sq_scr = B.sb["sq_scr"].ap()[:, 0:F]
    if kind == "ln":
        t = P.op("vector", lambda E: E.reduce_sum(tmp, xt_ap, axis=AX.X, negate=True),
                 deps=[dep], inc="v")
        t = P.op("vector", (lambda o=negmean, s=tmp:
                            lambda E: E.tensor_scalar(o, s, 1.0 / F, None, OP.mult))(),
                 deps=[t], inc="v")
        t = P.op("scalar", lambda E: E.activation(sq_scr, xt_ap, AF.Square, bias=negmean,
                                                  scale=1.0, accum_out=ssq),
                 deps=[t, dep], inc="s")
    else:
        negmean = None
        t = P.op("scalar", lambda E: E.activation(sq_scr, xt_ap, AF.Square, bias=0.0,
                                                  scale=1.0, accum_out=ssq),
                 deps=[dep], inc="s")
    t = P.op("scalar", lambda E: E.activation(tmp, ssq, AF.Ln, bias=eps, scale=1.0 / F),
             deps=[t], inc="s")
    t = P.op("scalar", lambda E: E.activation(rstd, tmp, AF.Exp, bias=0.0, scale=-0.5),
             deps=[t], inc="s")
    return negmean, rstd, t


def transpose_tiles(B, srcs, src_deps, dst_ap, dst_col0, tag,
                    extra_evac_dep=None, psT_state=None):
    P = B.P
    psT = B.psT.ap()
    ident = B.sb["ident_bf"].ap()
    if psT_state is None:
        psT_state = [None, None]
    n = len(srcs)
    last = None
    for g0 in range(0, n, 4):
        half = (g0 // 4) % 2
        toks = []
        cnt = min(4, n - g0)
        for j in range(cnt):
            i = g0 + j
            dstp = psT[:, half * 1024 + j * 128: half * 1024 + (j + 1) * 128]
            dd = src_deps[i]
            dd = list(dd) if isinstance(dd, list) else [dd]
            toks.append(P.op("tensor",
                             (lambda dstp=dstp, s=srcs[i]: lambda E: E.transpose(dstp, s, ident))(),
                             deps=dd + [psT_state[half]], inc="t" + tag))
        srcp = psT[:, half * 1024: half * 1024 + cnt * 128]
        dsta = dst_ap[:, dst_col0 + g0 * 128: dst_col0 + (g0 + cnt) * 128]
        psT_state[half] = P.op("vector",
                               (lambda dsta=dsta, srcp=srcp: lambda E: E.tensor_copy(dsta, srcp))(),
                               deps=[toks[-1], extra_evac_dep], inc="v" + tag)
        last = psT_state[half]
    return last, psT_state


def p0_constants(B):
    P = B.P
    T = B.T
    d = B.din
    ident = d("c_ident_bf", (128, 128), BF16)
    identf = d("c_ident_f32", (128, 128), F32)
    onesf = d("c_ones_f32", (1, 128), F32)
    onesbf = d("c_ones_bf", (1, 128), BF16)
    r127 = d("c_row127", (128, 128), F32)
    triu = d("c_triu", (128, 128), F32)

    names = [("ident_bf", ident, BF16, (128, 128)),
             ("ident_f32", identf, F32, (128, 128)),
             ("ones_f32", onesf, F32, (1, 128)),
             ("ones_bf", onesbf, BF16, (1, 128)),
             ("row127", r127, F32, (128, 128)),
             ("triu", triu, F32, (128, 128))]
    toks = []
    for nm, src, dt, shp in names:
        h = B.sbuf(nm, shp, dt)
        toks.append(P.dma("sync", h.ap(), src, inc="dc"))
    cp = d("c_pool_src", (128, 16), F32)
    toks.append(P.dma("sync", B._cpool.ap(), cp, inc="dc"))
    B.p0_tok = toks[-1]
    P.barrier()


def emit_ln_to_cm(B, x_dram, out_cm_dram, tag, kind="ln", eps=1e-5, gain_col=None,
                  also_tm_dram=None):
    P, T, NTB = B.P, B.T, B.NTB
    xbuf = B.sb["nx_xbuf"]
    stats = B.sb["nx_st"]
    xn_tm = B.sb["nx_tm"]
    stage = B.sb["nx_stage"]

    rd_tok = {}
    xn_toks = {}
    tm_dma = {}
    for tb in range(NTB):
        buf = (tb % 2)
        xa = xbuf.ap()[:, buf * D:(buf + 1) * D]
        ld = P.dma("sync", xa, x_dram[tb * 128:(tb + 1) * 128, :],
                   deps=[rd_tok.get(tb - 2), tm_dma.get(tb - 2)],
                   inc="dl%s%d" % (tag, buf))
        st = stats.ap()[:, buf * 4:buf * 4 + 4]
        negmean, rstd, t_st = norm_stats(B, xa, st, eps, ld, kind=kind)
        dst = xn_tm.ap()[:, tb * D:(tb + 1) * D]
        if also_tm_dram is not None:
            xf = B.sb["nx_f32"].ap()[:, buf * D:(buf + 1) * D]
            if kind == "ln":
                fn = (lambda o=xf, xa=xa, nm=negmean, r=rstd:
                      lambda E: E.tensor_scalar(o, xa, nm, r, OP.add, OP.mult))()
            else:
                fn = (lambda o=xf, xa=xa, r=rstd:
                      lambda E: E.tensor_scalar(o, xa, r, None, OP.mult))()
            tf = P.op("vector", fn, deps=[t_st, tm_dma.get(tb - 2), xn_toks.get(tb - 2)],
                      inc="v" + tag)
            rd_tok[tb] = tf
            cst = P.op("vector", (lambda dst=dst, xf=xf: lambda E: E.tensor_copy(dst, xf))(),
                       deps=[tf], inc="v" + tag)
            xn_toks[tb] = cst
            tm_dma[tb] = P.dma("gpsimd", also_tm_dram[tb * 128:(tb + 1) * 128, :], xf,
                               deps=[tf], inc="dg%s%d" % (tag, buf))
        else:
            if kind == "ln":
                fn = (lambda dst=dst, xa=xa, nm=negmean, r=rstd:
                      lambda E: E.tensor_scalar(dst, xa, nm, r, OP.add, OP.mult))()
            else:
                fn = (lambda dst=dst, xa=xa, r=rstd:
                      lambda E: E.tensor_scalar(dst, xa, r, None, OP.mult))()
            rd_tok[tb] = P.op("vector", fn, deps=[t_st], inc="v" + tag)
            xn_toks[tb] = rd_tok[tb]
    stage_dma = [None, None]
    pstate = None
    for mt in range(8):
        sg = mt % 2
        stg = stage.ap()[:, sg * T:(sg + 1) * T]
        srcs = [xn_tm.ap()[:, tb * D + mt * 128: tb * D + (mt + 1) * 128] for tb in range(NTB)]
        deps = [xn_toks[tb] for tb in range(NTB)]
        gdep = None
        last, pstate = transpose_tiles(B, srcs, deps, stg, 0, tag,
                                       extra_evac_dep=stage_dma[sg], psT_state=pstate)
        if gain_col is not None:
            gcol = gain_col.ap()[:, mt:mt + 1]
            last = P.op("vector",
                        (lambda stg=stg, gcol=gcol: lambda E: E.tensor_scalar(stg, stg, gcol, None, OP.mult))(),
                        deps=[last], inc="v" + tag)
        stage_dma[sg] = P.dma("gpsimd", out_cm_dram[mt * 128:(mt + 1) * 128, :], stg,
                              deps=[last], inc="dgs%s%d" % (tag, sg))
    P.barrier()


def p1(B, stack):
    B.phase_bufs(stack, {
        "nx_xbuf": ((128, 2 * D), F32),
        "sq_scr": ((128, 2048), F32),
        "nx_f32": ((128, 2 * D), F32),
        "nx_st": ((128, 8), F32),
        "nx_tm": ((128, B.NTB * D), BF16),
        "nx_stage": ((128, 2 * B.T), BF16),
    })
    x = B.din("x", (B.T, D), F32)
    xn_cm = B.dscratch("xn_cm", (D, B.T), BF16)
    emit_ln_to_cm(B, x, xn_cm, "p1", kind="ln", eps=1e-5)


def host_constants(T):
    c = {}
    c["c_ident_bf"] = np.eye(128, dtype=ml_dtypes.bfloat16)
    c["c_ident_f32"] = np.eye(128, dtype=np.float32)
    c["c_ones_f32"] = np.ones((1, 128), dtype=np.float32)
    c["c_ones_bf"] = np.ones((1, 128), dtype=ml_dtypes.bfloat16)
    c["c_negones_f32"] = -np.ones((1, 128), dtype=np.float32)
    s_ = np.arange(128)[:, None]
    t_ = np.arange(128)[None, :]
    c["c_trimask_add"] = np.where(s_ <= t_, 0.0, NEG_BIG).astype(np.float32)
    r127 = np.zeros((128, 128), dtype=np.float32)
    r127[127, :] = 1.0
    c["c_row127"] = r127
    cm = np.zeros((4, 128, 512), dtype=np.float32)
    for v in range(4):
        q = np.arange(128)[:, None] + v * 128
        k = np.arange(512)[None, :]
        cm[v] = np.where(k <= q, 0.0, NEG_BIG)
    c["c_causal_mask"] = np.ascontiguousarray(cm.transpose(1, 0, 2).reshape(128, 4 * 512)).astype(ml_dtypes.bfloat16)
    cmT = np.zeros((4, 128, 512), dtype=np.float32)
    for v in range(4):
        j = np.arange(128)[:, None] + v * 128
        i = np.arange(512)[None, :]
        cmT[v] = np.where(j <= i, 0.0, NEG_BIG)
    c["c_causal_maskT"] = np.ascontiguousarray(cmT.transpose(1, 0, 2).reshape(128, 4 * 512)).astype(ml_dtypes.bfloat16)
    c["c_triu"] = (s_ <= t_).astype(np.float32)
    c["c_cumsum_bf"] = (s_ <= t_).astype(ml_dtypes.bfloat16)
    W = 32
    NCk = max(T // 512, 1)
    pos = np.arange(T)[None, :]
    ang = (10000.0 ** (-np.arange(W) / W))[:, None]
    rad = pos * ang
    scale = 128.0 ** -0.5
    cosf = (np.cos(rad) * scale).astype(np.float32)
    sinf = (np.sin(rad) * scale).astype(np.float32)
    ncol = cdiv(NCk, 4) * 512
    for nm, tbl in (("cos", cosf), ("sin", sinf)):
        grp = [np.zeros((32, ncol), dtype=np.float32) for _ in range(4)]
        for ch in range(NCk):
            seg = tbl[:, ch * 512:(ch + 1) * 512]
            grp[ch % 4][:, (ch // 4) * 512:(ch // 4) * 512 + seg.shape[1]] = seg
        for gg in range(4):
            c["c_rope_%s_g%d" % (nm, gg)] = grp[gg].astype(ml_dtypes.bfloat16)
    return c


def host_prep(inputs, T):
    f = np.ascontiguousarray
    m = {}
    m.update(host_constants(T))
    m["w_wq"] = f(inputs["L0_attn_wq"])
    m["w_wk"] = f(inputs["L0_attn_wk"])
    m["w_wv"] = f(inputs["L0_attn_wv"])
    m["w_wo"] = f(inputs["L0_attn_wo"])
    m["w_Wr1"] = f(inputs["L0_ffn_Wr1"])
    m["w_br1"] = f(inputs["L0_ffn_br1"].reshape(2, 128).T)
    m["w_Wr2"] = f(inputs["L0_ffn_Wr2"])
    m["w_br2"] = f(inputs["L0_ffn_br2"].reshape(1, 8))
    m["w_Ws1"] = f(inputs["L0_ffn_Ws1"])
    m["w_bs1"] = f(inputs["L0_ffn_bs1"].reshape(32, 128).T)
    m["w_Ws2"] = f(inputs["L0_ffn_Ws2"])
    m["w_bs2"] = f(inputs["L0_ffn_bs2"].reshape(1, 1024))
    m["w_We1"] = f(inputs["L0_ffn_We1"])
    m["w_be1"] = f(inputs["L0_ffn_be1"].reshape(8, 32, 128).transpose(2, 0, 1).reshape(128, 8 * 32))
    m["w_We2"] = f(inputs["L0_ffn_We2"])
    m["w_be2"] = f(inputs["L0_ffn_be2"].reshape(8, 1, 1024))
    m["w_ipT"] = f(inputs["L1_attn_in_proj_w"].T)
    m["w_convw"] = f(inputs["L1_attn_conv_w"].reshape(18, 128, 4).transpose(1, 0, 2).reshape(128, 72))
    m["w_convb"] = f(inputs["L1_attn_conv_b"].reshape(18, 128).T)
    m["w_dtb"] = f(inputs["L1_attn_dt_bias"].reshape(1, 32))
    m["w_Alog"] = f(inputs["L1_attn_A_log"].reshape(1, 32))
    m["w_Dh"] = f(inputs["L1_attn_D"].reshape(1, 32))
    m["w_normw"] = f(inputs["L1_attn_norm_w"].reshape(16, 128).T)
    m["w_opT"] = f(inputs["L1_attn_out_proj_w"].T)
    m["w_Dexp128"] = f(np.broadcast_to(np.repeat(inputs["L1_attn_D"].reshape(-1), 64)[None, :],
                                       (128, 2048)).astype(ml_dtypes.bfloat16))
    ind4 = np.zeros((4, 512), dtype=np.float32)
    for hh in range(4):
        ind4[hh, hh * 128:(hh + 1) * 128] = 1.0
    m["c_ind4"] = ind4
    ind64 = np.zeros((32, 2048), dtype=np.float32)
    for hh in range(32):
        ind64[hh, hh * 64:(hh + 1) * 64] = 1.0
    m["c_ind64"] = ind64.astype(ml_dtypes.bfloat16)
    m["w_g1"] = f(inputs["L1_ffn_g1"].reshape(8, 128).T)
    m["w_g2"] = f(inputs["L1_ffn_g2"].reshape(8, 128).T)
    m["w_WqT"] = f(inputs["L1_ffn_Wq"].T)
    m["w_WkcT"] = f(inputs["L1_ffn_Wkc"].T)
    m["w_WvcT"] = f(inputs["L1_ffn_Wvc"].T)
    m["w_WoT"] = f(inputs["L1_ffn_Wo"].T)
    m["w_WrT"] = f(inputs["L1_ffn_Wr"].T)
    m["w_eb"] = f(inputs["L1_ffn_expert_bias"].reshape(1, 8))
    m["w_sW1T"] = f(inputs["L1_ffn_sW1"].T)
    m["w_sW3T"] = f(inputs["L1_ffn_sW3"].T)
    m["w_sW2T"] = f(inputs["L1_ffn_sW2"].T)
    m["w_eW1T"] = f(inputs["L1_ffn_eW1"].transpose(0, 2, 1))
    m["w_eW3T"] = f(inputs["L1_ffn_eW3"].transpose(0, 2, 1))
    m["w_eW2T"] = f(inputs["L1_ffn_eW2"].transpose(0, 2, 1))
    xs = [np.ascontiguousarray(inputs["L0_attn_x"][b]) for b in range(inputs["L0_attn_x"].shape[0])]
    return m, xs


import contextlib

PHASE_FNS = {}


def build_program(T, phases, force_out=()):
    B = Builder(T=T, phases=phases)
    B.force_out = tuple(force_out)
    with contextlib.ExitStack() as stack:
        p0_constants(B)
        for ph in phases:
            with contextlib.ExitStack() as ps:
                PHASE_FNS[ph](B, ps)
    B.P.emit()
    return B


def rope_tbl(B, nm, ch):
    h = B.sb["rope_%s_g%d" % (nm, ch % 4)]
    col = (ch // 4) * 512
    return h.ap()[0:32, col:col + 512]


def p2(B, stack):
    import contextlib
    P, T, NTB = B.P, B.T, B.NTB
    NH = 8
    NC = T // 512
    xn_cm = B.dscratch("xn_cm", (D, T), BF16)
    s_hbm = B.dscratch("s_hbm", (T, D), F32)
    o_hbm = B.dscratch("o_hbm", (NTB, NH, 128, 128), BF16)
    ncol = cdiv(max(T // 512, 1), 4) * 512
    cm_src = B.din("c_causal_mask", (128, 4 * 512), BF16)
    cmT_src = B.din("c_causal_maskT", (128, 4 * 512), BF16)
    rope_srcs = [B.din("c_rope_%s_g%d" % (nm, gg), (32, ncol), BF16)
                 for nm in ("cos", "sin") for gg in range(4)]
    B.phase_bufs(stack, dict([("causal_mask", ((128, 4 * 512), BF16)),
                              ("causal_maskT", ((128, 4 * 512), BF16))] +
                             [("rope_%s_g%d" % (nm, gg), ((32, ncol), BF16))
                              for nm in ("cos", "sin") for gg in range(4)]))
    B.P.dma("sync", B.sb["causal_mask"].ap(), cm_src, inc="dc2")
    B.P.dma("sync", B.sb["causal_maskT"].ap(), cmT_src, inc="dc2")
    for i, (nm, gg) in enumerate((n, g) for n in ("cos", "sin") for g in range(4)):
        B.P.dma("sync", B.sb["rope_%s_g%d" % (nm, gg)].ap(), rope_srcs[i], inc="dc2")
    B.P.barrier()
    wq = B.din("w_wq", (D, NH, 128), F32)
    wk = B.din("w_wk", (D, NH, 128), F32)
    wv = B.din("w_wv", (D, NH, 128), F32)
    wo = B.din("w_wo", (NH, 128, D), F32)

    psA, psB, psC = B.psA.ap(), B.psB.ap(), B.psC.ap()
    psD = psC
    identf = B.sb["ident_f32"].ap()
    onesf = B.sb["ones_f32"].ap()
    maskq = B.sb["causal_mask"].ap()
    maskj = B.sb["causal_maskT"].ap()

    for v in (-1.0, 0.08838834764831845):
        B.constf(v)
    gam = stack.enter_context(B.nc.sbuf_tensor("p2_gam", [128, NH * NTB], F32))

    with contextlib.ExitStack() as s1:
        B.phase_bufs(s1, {
            "p2_xn": ((128, 8 * T), BF16),
            "p2_qT": ((128, 2 * T), BF16),
            "p2_kT": ((128, 2 * T), BF16),
            "p2_v": ((128, 2 * T), BF16),
            "p2_pT": ((128, NTB * 512), BF16),
            "p2_rt": ((128, 6 * 512), F32),
            "p2_scr": ((128, 2 * 1024), F32),
            "p2_st": ((128, 256), F32),
            "p2_wqkv": ((128, 3 * 1024), BF16),
            "p2_ostg": ((128, 2 * 512), BF16),
        })
        xn = B.sb["p2_xn"].ap().rearrange("p (a t) -> p a t", a=8)
        qT2 = B.sb["p2_qT"].ap()
        kT2 = B.sb["p2_kT"].ap()
        vv2 = B.sb["p2_v"].ap()
        pT = B.sb["p2_pT"].ap()
        rt = B.sb["p2_rt"].ap()
        scr = B.sb["p2_scr"].ap()
        st2 = B.sb["p2_st"].ap()
        wsb = B.sb["p2_wqkv"].ap().rearrange("p (w c) -> p w c", w=3)
        ostg = B.sb["p2_ostg"].ap()

        head_done = None
        prev_scores_done = {}
        for h in range(NH):
            tg = "2h%d" % (h % 2)
            hb2 = h % 2
            qT = qT2[:, hb2 * T:(hb2 + 1) * T]
            kT = kT2[:, hb2 * T:(hb2 + 1) * T]
            vv = vv2[:, hb2 * T:(hb2 + 1) * T]
            st = st2[:, hb2 * 128:(hb2 + 1) * 128]
            if h == 0:
                for mt in range(8):
                    xn_tok = P.dma("sync", xn[:, mt, :], xn_cm[mt * 128:(mt + 1) * 128, :],
                                   deps=[], inc="dxn0")
            w_tok = None
            for wi, wdr in enumerate((wq, wk, wv)):
                dst = wsb[:, wi, :].rearrange("p (a d) -> p a d", a=8)
                src = wdr[:, h, :].rearrange("(a p) d -> p a d", p=128)
                w_tok = P.dma("gpsimd", dst, src,
                              deps=[prev_scores_done.get("qkv_t")], inc="dwq%d" % (h % 2))

            prev2 = prev_scores_done.get(("head", h - 2), [])
            ev = {}
            for c in range(NC):
                psq = psA[:, 0:512]
                psk = psA[:, 512:1024]
                mm = {}
                for wi in range(2):
                    psx = psq if wi == 0 else psk
                    for mt in range(8):
                        dep = [xn_tok, w_tok] + (prev2 if (c == 0 and mt == 0) else [])
                        if mt == 0:
                            dep += [ev.get((2, c - 1)) if wi == 0 else ev.get((0, c)),
                                    ev.get((wi, c - 1, "s")), ev.get((wi, c - 1))]
                        mm[wi] = P.op("tensor",
                                      (lambda o=psx, w=wsb[:, wi, mt * 128:(mt + 1) * 128],
                                       x=xn[:, mt, c * 512:(c + 1) * 512],
                                       s0=(mt == 0), s1=(mt == 7):
                                       lambda E: E.matmul(o, w, x, start=s0, stop=s1))(),
                                      deps=dep, inc=("t" + tg) if mt == 7 else None)
                for wi, dst in ((0, qT), (1, kT)):
                    psx = psq if wi == 0 else psk
                    cosa = rope_tbl(B, "cos", c)
                    sina = rope_tbl(B, "sin", c)
                    evn = psx[0:32, :]
                    odd = psx[32:64, :]
                    par = c % 2
                    r0 = rt[0:32, par * 1536:par * 1536 + 512]
                    r1 = rt[0:32, par * 1536 + 512:par * 1536 + 1024]
                    r2 = rt[0:32, par * 1536 + 1024:par * 1536 + 1536]
                    dep = [mm[wi], ev.get((0, c - 2)), ev.get((1, c - 2)),
                           ev.get((0, c - 2, "s")), ev.get((1, c - 2, "s"))]
                    if wi == 1:
                        dep.append(ev.get((0, c)))
                    t0 = P.op("vector", (lambda o=r2, s=odd: lambda E: E.tensor_copy(o, s))(),
                              deps=dep + (prev2 if c == 0 else []), inc="v" + tg)
                    t1 = P.op("vector", (lambda o=r0, a=evn, b=cosa: lambda E: E.tensor_mul(o, a, b))(), deps=dep, inc="v" + tg)
                    t2 = P.op("vector", (lambda o=r1, a=r2, b=sina: lambda E: E.tensor_mul(o, a, b))(), deps=[t0], inc="v" + tg)
                    o0 = dst[0:32, c * 512:(c + 1) * 512]
                    t3 = P.op("vector", (lambda o=o0, a=r0, b=r1: lambda E: E.tensor_sub(o, a, b))(), deps=[t1, t2], inc="v" + tg)
                    t4 = P.op("vector", (lambda o=r0, a=evn, b=sina: lambda E: E.tensor_mul(o, a, b))(), deps=[t3], inc="v" + tg)
                    t5 = P.op("vector", (lambda o=r1, a=r2, b=cosa: lambda E: E.tensor_mul(o, a, b))(), deps=[t3], inc="v" + tg)
                    o1 = dst[32:64, c * 512:(c + 1) * 512]
                    t6 = P.op("vector", (lambda o=o1, a=r0, b=r1: lambda E: E.tensor_add(o, a, b))(), deps=[t4, t5], inc="v" + tg)
                    o2 = dst[64:128, c * 512:(c + 1) * 512]
                    t7 = P.op("vector", (lambda o=o2, s=psx[64:128, :]:
                                         lambda E: E.tensor_scalar(o, s, 0.08838834764831845, None, OP.mult))(),
                              deps=[mm[wi]] + (prev2 if c == 0 else []), inc="v" + tg)
                    ev[(wi, c)] = t6
                    ev[(wi, c, "s")] = t7
                mmv = None
                for mt in range(8):
                    dep = [xn_tok, w_tok]
                    if mt == 0:
                        dep += [ev[(0, c)], ev[(0, c, "s")]]
                    mmv = P.op("tensor",
                               (lambda o=psq, w=wsb[:, 2, mt * 128:(mt + 1) * 128],
                                x=xn[:, mt, c * 512:(c + 1) * 512],
                                s0=(mt == 0), s1=(mt == 7):
                                lambda E: E.matmul(o, w, x, start=s0, stop=s1))(),
                               deps=dep, inc=("t" + tg) if mt == 7 else None)
                ev[(2, c)] = P.op("vector",
                                  (lambda d=pT[:, c * 512:(c + 1) * 512], s=psq:
                                   lambda E: E.tensor_copy(d, s))(),
                                  deps=[mmv, ev.get((2, c - 2)),
                                        prev_scores_done.get("o_t") if c < 2 else None],
                                  inc="v" + tg)
            qkv_v = ev[(2, NC - 1)]
            qkv_s = ev[(1, NC - 1, "s")]
            prev_scores_done["qkv_t"] = P.op("sync", lambda E: E.nop(),
                                             deps=[qkv_v, qkv_s], inc="q2")
            srcs = [pT[:, tb * 128:(tb + 1) * 128] for tb in range(NTB)]
            sdeps = [[ev[(2, tb // 4)]] for tb in range(NTB)]
            vt_done, _ = transpose_tiles(B, srcs, sdeps, vv, 0, tg)

            negm_c = st[:, 0:8]
            Zc = st[:, 8:16]
            nm_q = st[:, 16:17]
            e_sc = st[:, 18:26]
            negMh = st[:, 26:27]
            Mh = st[:, 27:28]
            nma = st[:, 32:32 + NTB]
            za = st[:, 64:64 + NTB]
            lnZa = st[:, 96:96 + NTB]
            scrh = scr[:, hb2 * 1024:(hb2 + 1) * 1024]
            bank_tok = {}
            comb_v = {}
            comb_s = {}
            for qb in range(NTB):
                nchunk = qb // 4 + 1
                for c in range(nchunk):
                    par = c % 2
                    psx = psA[:, par * 512:(par + 1) * 512]
                    dep = [qkv_v, qkv_s, bank_tok.get(par)]
                    mmx = P.op("tensor",
                               (lambda o=psx, w=qT[:, qb * 128:(qb + 1) * 128],
                                x=kT[:, c * 512:(c + 1) * 512]:
                                lambda E: E.matmul(o, w, x, start=True, stop=True))(),
                               deps=dep, inc="t" + tg)
                    if c == qb // 4:
                        vnt = qb % 4
                        mmx = P.op("vector",
                                   (lambda o=psx, m=maskq[:, vnt * 512:(vnt + 1) * 512]:
                                    lambda E: E.tensor_add(o, o, m))(),
                                   deps=[mmx], inc="v" + tg)
                    t_mx = P.op("vector",
                                (lambda o=negm_c[:, c:c + 1], s=psx:
                                 lambda E: E.reduce_max(o, s, axis=AX.X, negate=True))(),
                                deps=[mmx, comb_v.get(qb - 1), comb_s.get(qb - 1)]
                                + (prev2 if qb == 0 else []), inc="v" + tg)
                    tok = P.op("scalar",
                               (lambda o=scrh[:, par * 512:(par + 1) * 512], s=psx,
                                b=negm_c[:, c:c + 1], z=Zc[:, c:c + 1]:
                                lambda E: E.activation(o, s, AF.Exp, bias=b, scale=1.0, accum_out=z))(),
                               deps=[t_mx] + (prev2 if qb == 0 else []), inc="s" + tg)
                    bank_tok[par] = tok
                dps = [bank_tok.get(0), bank_tok.get(1)]
                t1 = P.op("vector", (lambda o=nm_q, s=negm_c[:, 0:nchunk]:
                                     lambda E: E.tensor_reduce(o, s, AX.X, OP.min))(),
                          deps=dps, inc="v" + tg)
                t2 = P.op("scalar", (lambda o=e_sc[:, 0:nchunk], s=negm_c[:, 0:nchunk], b=nm_q:
                                     lambda E: E.activation(o, s, AF.Exp, bias=b, scale=-1.0))(),
                          deps=[t1], inc="s" + tg)
                t3 = P.op("vector", (lambda o=e_sc[:, 0:nchunk], a=Zc[:, 0:nchunk]:
                                     lambda E: E.tensor_mul(o, o, a))(),
                          deps=[t2], inc="v" + tg)
                t4 = P.op("vector", (lambda o=za[:, qb:qb + 1], s=e_sc[:, 0:nchunk]:
                                     lambda E: E.reduce_sum(o, s, axis=AX.X))(),
                          deps=[t3], inc="v" + tg)
                t5 = P.op("vector", (lambda o=nma[:, qb:qb + 1], s=nm_q: lambda E: E.tensor_copy(o, s))(),
                          deps=[t1], inc="v" + tg)
                comb_v[qb] = t5
                comb_s[qb] = t2
            t1 = P.op("vector", (lambda o=negMh, s=nma: lambda E: E.tensor_reduce(o, s, AX.X, OP.min))(),
                      deps=[comb_v[NTB - 1]], inc="v" + tg)
            t2 = P.op("tensor", (lambda o=psA[0:1, 0:128], s=negMh:
                                 lambda E: E.transpose(o, s, identf))(),
                      deps=[t1, bank_tok.get(0)], inc="t" + tg)
            t3 = P.op("vector", (lambda o=scrh[0:1, 0:128], s=psA[0:1, 0:128]:
                                 lambda E: E.tensor_copy(o, s))(), deps=[t2], inc="v" + tg)
            t4 = P.op("vector", (lambda o=scrh[0:1, 128:129], s=scrh[0:1, 0:128]:
                                 lambda E: E.tensor_reduce(o, s, AX.X, OP.min))(),
                      deps=[t3], inc="v" + tg)
            t5 = P.op("tensor", (lambda o=psA[:, 128:129], w=onesf, x=scrh[0:1, 128:129]:
                                 lambda E: E.matmul(o, w, x, start=True, stop=True))(),
                      deps=[t4], inc="t" + tg)
            t6 = P.op("vector", (lambda o=negMh, s=psA[:, 128:129]:
                                 lambda E: E.tensor_copy(o, s))(), deps=[t5, t2], inc="v" + tg)
            t7 = P.op("scalar", (lambda o=Mh, s=negMh: lambda E: E.mul(o, s, -1.0))(),
                      deps=[t6], inc="s" + tg)
            t8 = P.op("scalar", (lambda o=lnZa, s=za: lambda E: E.activation(o, s, AF.Ln))(),
                      deps=[comb_v[NTB - 1]], inc="s" + tg)
            tA = P.op("vector", (lambda o=nma, a=nma, sc=Mh, b=lnZa:
                                 lambda E: E.scalar_tensor_tensor(o, a, sc, b, OP.add, OP.subtract))(),
                      deps=[t7, t8], inc="v" + tg)
            gslice = gam.ap()[:, h * NTB:(h + 1) * NTB]
            tG = P.op("scalar", (lambda o=gslice, s=nma: lambda E: E.activation(o, s, AF.Exp))(),
                      deps=[tA], inc="s" + tg)

            NS = T // 512
            o_mm_done = {}
            o_ev = {}
            exp_tok = {}
            pso_prev = prev_scores_done.get("s_last")
            poev = [prev_scores_done.get("oev0"), prev_scores_done.get("oev1")]
            for s in range(NS):
                jmax = 4 * s + 4
                opar = s % 2
                pso = psC[:, opar * 512:(opar + 1) * 512]
                for jb in range(jmax):
                    par = jb % 2
                    psx = psB[:, par * 512:(par + 1) * 512]
                    dep = [qkv_v, qkv_s, exp_tok.get((s, jb - 2)),
                           o_mm_done.get(s - 1) if jb < 2 else None,
                           pso_prev if (s == 0 and jb < 2) else None]
                    mmx = P.op("tensor",
                               (lambda o=psx, w=kT[:, jb * 128:(jb + 1) * 128],
                                x=qT[:, s * 512:(s + 1) * 512]:
                                lambda E: E.matmul(o, w, x, start=True, stop=True))(),
                               deps=dep, inc="t" + tg)
                    if jb >= 4 * s:
                        vnt = jb - 4 * s
                        mmx = P.op("vector",
                                   (lambda o=psx, m=maskj[:, vnt * 512:(vnt + 1) * 512]:
                                    lambda E: E.tensor_add(o, o, m))(),
                                   deps=[mmx], inc="v" + tg)
                    wdep = [mmx, t6, o_mm_done.get(s - 1), vt_done if s == 0 else None,
                            prev_scores_done.get("o_t") if s == 0 else None]
                    exp_tok[(s, jb)] = P.op("scalar",
                                            (lambda o=pT[:, jb * 512:(jb + 1) * 512], ss=psx, b=negMh:
                                             lambda E: E.activation(o, ss, AF.Exp, bias=b))(),
                                            deps=wdep, inc="s" + tg)
                    o_mm_done[s] = P.op("tensor",
                                        (lambda o=pso, w=vv[:, jb * 128:(jb + 1) * 128],
                                         x=pT[:, jb * 512:(jb + 1) * 512],
                                         s0=(jb == 0), s1=(jb == jmax - 1):
                                         lambda E: E.matmul(o, w, x, start=s0, stop=s1))(),
                                        deps=[exp_tok[(s, jb)], vt_done, o_ev.get(s - 2),
                                              poev[opar] if s < 2 else None],
                                        inc="t" + tg)
                sg = s % 2
                stg = ostg[:, sg * 512:(sg + 1) * 512]
                tev = P.op("vector", (lambda o=stg, ss=pso: lambda E: E.tensor_copy(o, ss))(),
                           deps=[o_mm_done[s], o_ev.get(s - 2), poev[sg] if s < 2 else None],
                           inc="v" + tg)
                dst = o_hbm[4 * s:4 * s + 4, h, :, :].rearrange("b p t -> p b t")
                o_ev[s] = P.dma("gpsimd", dst, stg.rearrange("p (b t) -> p b t", b=4),
                                deps=[tev], inc="do%d" % sg)
            prev_scores_done["o_t"] = o_mm_done[NS - 1]
            prev_scores_done["s_last"] = exp_tok[(NS - 1, 4 * (NS - 1) + 3)]
            prev_scores_done["oev0"] = o_ev[NS - 2] if NS >= 2 else o_ev[NS - 1]
            prev_scores_done["oev1"] = o_ev[NS - 1]
            prev_scores_done[("head", h)] = [o_mm_done[NS - 1],
                                             prev_scores_done["s_last"], tev, tG,
                                             comb_s[NTB - 1]]
    P.barrier()

    with contextlib.ExitStack() as s2:
        B.phase_bufs(s2, {
            "p2_wo": ((128, NH * D), BF16),
            "p2_ob": ((128, 2 * NH * 128), BF16),
            "p2_acc": ((128, 2 * D), F32),
            "p2_sst": ((128, 2 * D), F32),
        })
        wosb = B.sb["p2_wo"].ap().rearrange("p (h c) -> p h c", h=NH)
        ob = B.sb["p2_ob"].ap().rearrange("p (b h t) -> p b h t", b=2, h=NH)
        acc = B.sb["p2_acc"].ap()
        sst = B.sb["p2_sst"].ap()
        wo_tok = None
        for h in range(NH):
            wo_tok = P.dma("gpsimd", wosb[:, h, :], wo[h], inc="dwo")
        ld = {}
        st_dma = {}
        accw = {}
        mmlast = {}
        bank_prev = {0: None, 1: None}
        for tb in range(NTB):
            buf = tb % 2
            ld[tb] = P.dma("sync", ob[:, buf, :, :],
                           o_hbm[tb].rearrange("h p t -> p h t"),
                           deps=[accw.get(tb - 2), mmlast.get(tb - 2)], inc="dob%d" % buf)
            acc_a = acc[:, buf * D:(buf + 1) * D]
            hdep = None
            for h in range(NH):
                ppar = (tb * NH + h) % 2
                psx = (psA if ppar == 0 else psB)
                mm1 = P.op("tensor",
                           (lambda o=psx[:, 0:512], w=ob[:, buf, h, :], x=wosb[:, h, 0:512]:
                            lambda E: E.matmul(o, w, x, start=True, stop=True))(),
                           deps=[ld[tb], wo_tok, bank_prev[ppar]], inc=None)
                mm2 = P.op("tensor",
                           (lambda o=psx[:, 512:1024], w=ob[:, buf, h, :], x=wosb[:, h, 512:1024]:
                            lambda E: E.matmul(o, w, x, start=True, stop=True))(),
                           deps=[], inc="tp")
                gcol = gam.ap()[:, h * NTB + tb:h * NTB + tb + 1]
                if h == 0:
                    accw[(tb, h)] = P.op("vector",
                                         (lambda o=acc_a, s=psx, g=gcol:
                                          lambda E: E.tensor_scalar(o, s, g, None, OP.mult))(),
                                         deps=[mm2, st_dma.get(tb - 2)], inc="vp")
                else:
                    accw[(tb, h)] = P.op("vector",
                                         (lambda o=acc_a, s=psx, g=gcol, a=acc_a:
                                          lambda E: E.scalar_tensor_tensor(o, s, g, a, OP.mult, OP.add))(),
                                         deps=[mm2, accw[(tb, h - 1)]], inc="vp")
                bank_prev[ppar] = accw[(tb, h)]
            accw[tb] = accw[(tb, NH - 1)]
            mmlast[tb] = mm2
            st_dma[tb] = P.dma("gpsimd", s_hbm[tb * 128:(tb + 1) * 128, :], acc_a,
                               deps=[accw[tb]], inc="dsb%d" % buf)
    P.barrier()


def p3(B, stack):
    import contextlib
    P, T, NTB = B.P, B.T, B.NTB
    NE = 8
    GT = min(1024, T)
    NG = T // GT
    NTS = GT // 128
    I = 4096
    NIT = I // 128
    NR = 4
    IPR = NIT // NR
    xn_cm = B.dram["xn_cm"]
    s_hbm = B.dram["s_hbm"]
    x = B.dram["x"]
    x1_hbm = B.dscratch("x1_hbm", (T, D), F32)
    wr1 = B.din("w_Wr1", (D, 256), F32)
    br1 = B.din("w_br1", (128, 2), F32)
    wr2 = B.din("w_Wr2", (256, NE), F32)
    br2 = B.din("w_br2", (1, NE), F32)
    ws1 = B.din("w_Ws1", (D, I), F32)
    bs1 = B.din("w_bs1", (128, 32), F32)
    ws2 = B.din("w_Ws2", (I, D), F32)
    bs2 = B.din("w_bs2", (1, D), F32)
    we1 = B.din("w_We1", (NE, D, I), F32)
    be1 = B.din("w_be1", (128, NE * 32), F32)
    we2 = B.din("w_We2", (NE, I, D), F32)
    be2 = B.din("w_be2", (NE, 1, D), F32)

    psA, psB, psC = B.psA.ap(), B.psB.ap(), B.psC.ap()
    psD = psA
    onesb = B.sb["ones_bf"].ap()

    w_sb = stack.enter_context(B.nc.sbuf_tensor("p3_w", [128, NTB * 8], F32))

    with contextlib.ExitStack() as s1:
        B.phase_bufs(s1, {
            "p3_rxn": ((128, 2 * 8 * 512), BF16),
            "p3_hr": ((128, 2 * T), BF16),
            "p3_wr": ((128, 8 * 256 + 256), BF16),
            "p3_bias": ((128, 16), F32),
            "p3_rows": ((1, 64), BF16),
            "p3_lg": ((128, 64), F32),
        })
        rxn = B.sb["p3_rxn"].ap().rearrange("p (b a t) -> p b a t", b=2, a=8)
        hr = B.sb["p3_hr"].ap().rearrange("p (c t) -> p c t", c=2)
        wr = B.sb["p3_wr"].ap()
        wr1t = wr[:, 0:2048].rearrange("p (a c) -> p a c", a=8)
        wr2t = wr[:, 2048:2048 + 16].rearrange("p (c e) -> p c e", c=2)
        br1c = B.sb["p3_bias"].ap()
        br2r = B.sb["p3_rows"].ap()
        lg = B.sb["p3_lg"].ap()

        P.dma("gpsimd", wr1t, wr1.rearrange("(a p) c -> p a c", p=128), inc="dw3a")
        P.dma("gpsimd", wr2t, wr2.rearrange("(c p) e -> p c e", p=128), inc="dw3a")
        tb1 = P.dma("sync", br1c[:, 0:2], br1, inc="db3")
        tall = P.dma("gpsimd", br2r[0:1, 0:NE], br2, inc="dw3a")
        tw1 = tw2 = tb2 = tall

        NCk = T // 512
        hr_tok = {}
        ld_tok = {}
        for c in range(NCk):
            buf = c % 2
            ld_tok[c] = P.dma("sync", rxn[:, buf, :, :],
                              xn_cm.rearrange("(a p) t -> p a t", p=128)[:, :, c * 512:(c + 1) * 512],
                              deps=[hr_tok.get((1, c - 2))], inc="dx3%d" % buf)
            for ct in range(2):
                psx = psC[:, ct * 512:(ct + 1) * 512]
                mmt = None
                for mt in range(8):
                    mmt = P.op("tensor",
                               (lambda o=psx, w=wr1t[:, mt, ct * 128:(ct + 1) * 128],
                                xx=rxn[:, buf, mt, :], s0=(mt == 0), s1=(mt == 7):
                                lambda E: E.matmul(o, w, xx, start=s0, stop=s1))(),
                               deps=[ld_tok[c], tw1, hr_tok.get((ct, c - 1))],
                               inc=("t3" if mt == 7 else None))
                hr_tok[(ct, c)] = P.op("scalar",
                                       (lambda o=hr[:, ct, c * 512:(c + 1) * 512], s=psx,
                                        b=br1c[:, ct:ct + 1]:
                                        lambda E: E.activation(o, s, AF.Relu, bias=b))(),
                                       deps=[mmt, tb1], inc="s3")
        hr_done = hr_tok[(1, NCk - 1)]
        prev = {}
        for tb in range(NTB):
            dep = [hr_done, tw2, tb2, prev.get(tb - 1)]
            mmt = None
            for ct in range(2):
                mmt = P.op("tensor",
                           (lambda o=psD[:, 0:NE], w=hr[:, ct, tb * 128:(tb + 1) * 128],
                            xx=wr2t[:, ct, :], s0=(ct == 0), s1=False:
                            lambda E: E.matmul(o, w, xx, start=s0, stop=s1))(),
                           deps=dep + [prev.get(tb - 2)], inc=None)
            mmt = P.op("tensor",
                       (lambda o=psD[:, 0:NE], w=onesb[0:1, :], xx=br2r[0:1, 0:NE]:
                        lambda E: E.matmul(o, w, xx, start=False, stop=True))(),
                       deps=[], inc="t3")
            c0 = tb % 2
            gat = lg[:, c0 * 32:c0 * 32 + 8]
            ex = lg[:, c0 * 32 + 8:c0 * 32 + 16]
            m8 = lg[:, c0 * 32 + 16:c0 * 32 + 24]
            sc = lg[:, c0 * 32 + 24:c0 * 32 + 32]
            t1 = P.op("vector", (lambda o=sc[:, 0:1], s=psD[:, 0:NE]:
                                 lambda E: E.reduce_max(o, s, axis=AX.X, negate=True))(),
                      deps=[mmt, prev.get(tb - 2)], inc="v3")
            t2 = P.op("scalar", (lambda o=ex, s=psD[:, 0:NE], b=sc[:, 0:1], z=sc[:, 1:2]:
                                 lambda E: E.activation(o, s, AF.Exp, bias=b, accum_out=z))(),
                      deps=[t1], inc="s3")
            t3 = P.op("vector", (lambda o=sc[:, 2:3], s=sc[:, 1:2]: lambda E: E.reciprocal(o, s))(),
                      deps=[t2], inc="v3")
            t4 = P.op("vector", (lambda o=gat, s=ex, r=sc[:, 2:3]:
                                 lambda E: E.tensor_scalar(o, s, r, None, OP.mult))(),
                      deps=[t3], inc="v3")
            t5 = P.op("vector", (lambda o=m8, s=gat: lambda E: E.max(o, s))(),
                      deps=[t4], inc="v3")
            t6 = P.op("scalar", (lambda o=sc[:, 3:4], s=m8[:, 0:1]:
                                 lambda E: E.activation(o, s, AF.Exp, scale=0.5))(),
                      deps=[t5], inc="s3")
            t7 = P.op("scalar", (lambda o=sc[:, 4:5], s=m8[:, 1:2]:
                                 lambda E: E.activation(o, s, AF.Exp, scale=0.5))(),
                      deps=[t5], inc="s3")
            t8 = P.op("vector", (lambda o=sc[:, 5:6], a=sc[:, 3:4], b=sc[:, 4:5]:
                                 lambda E: E.tensor_add(o, a, b))(),
                      deps=[t6, t7], inc="v3")
            t9 = P.op("vector", (lambda o=sc[:, 5:6], s=sc[:, 5:6]: lambda E: E.reciprocal(o, s))(),
                      deps=[t8], inc="v3")
            tA = P.op("vector", (lambda o=sc[:, 6:7], a=sc[:, 3:4], b=sc[:, 5:6]:
                                 lambda E: E.tensor_mul(o, a, b))(),
                      deps=[t9], inc="v3")
            tB = P.op("vector", (lambda o=sc[:, 7:8], a=sc[:, 4:5], b=sc[:, 5:6]:
                                 lambda E: E.tensor_mul(o, a, b))(),
                      deps=[t9], inc="v3")
            tC = P.op("vector", (lambda o=ex, s=gat, m=m8[:, 0:1]:
                                 lambda E: E.tensor_scalar(o, s, m, None, OP.is_equal))(),
                      deps=[tA, t4], inc="v3")
            tD = P.op("vector", (lambda o=ex, s=ex, w=sc[:, 6:7]:
                                 lambda E: E.tensor_scalar(o, s, w, None, OP.mult))(),
                      deps=[tC], inc="v3")
            tE = P.op("vector", (lambda o=m8, s=gat, m=m8[:, 1:2]:
                                 lambda E: E.tensor_scalar(o, s, m, None, OP.is_equal))(),
                      deps=[tD, tB], inc="v3")
            tF = P.op("vector", (lambda o=w_sb.ap()[:, tb * 8:(tb + 1) * 8], s=m8,
                                 w=sc[:, 7:8], a=ex:
                                 lambda E: E.scalar_tensor_tensor(o, s, w, a, OP.mult, OP.add))(),
                      deps=[tE], inc="v3")
            prev[tb] = tF
    P.barrier()

    with contextlib.ExitStack() as s2:
        B.phase_bufs(s2, {
            "p3_xn": ((128, 8 * GT), BF16),
            "p3_h": ((128, 2 * IPR * GT), BF16),
            "p3_w1": ((128, 2 * IPR * 1024), BF16),
            "p3_w2": ((128, 2 * IPR * 1024), BF16),
            "p3_acc": ((128, NTS * D), F32),
            "p3_b1": ((128, 9 * 32), F32),
            "p3_b2": ((1, 2 * D), BF16),
            "p3_xs": ((128, 2 * 2 * D), F32),
            "p3_eps": ((128, 2 * D), F32),
        })
        xng = B.sb["p3_xn"].ap().rearrange("p (a t) -> p a t", a=8)
        hbuf = B.sb["p3_h"].ap().rearrange("p (b i t) -> p b i t", b=2, i=IPR)
        w1b = B.sb["p3_w1"].ap().rearrange("p (b a c) -> p b a c", b=2, a=8)
        w2b = B.sb["p3_w2"].ap().rearrange("p (b i c) -> p b i c", b=2, i=IPR)
        acc = B.sb["p3_acc"].ap().rearrange("p (s c) -> p s c", s=NTS)
        b1c = B.sb["p3_b1"].ap()
        b2r = B.sb["p3_b2"].ap()
        xs = B.sb["p3_xs"].ap().rearrange("p (b k c) -> p b k c", b=2, k=2)
        gps = B.sb["p3_eps"].ap().rearrange("p (b c) -> p b c", b=2)

        P.dma("sync", b1c[:, 0:32], bs1, inc="db3b")
        tb1 = P.dma("sync", b1c[:, 32:32 + 256], be1, deps=[], inc="db3b")

        NRBK = 9 * NR
        xn_ld = {}
        h_ev = {}
        out_mm = {}
        psC_tok = {}
        pAB_tok = {}
        acc_free = {}
        acc_tok = {}
        ep_tok = {}
        st_dma = {}
        for g in range(NG):
            gb = g % 2
            xn_ld[g] = P.dma("sync", xng,
                             xn_cm.rearrange("(a p) t -> p a t", p=128)[:, :, g * GT:(g + 1) * GT],
                             deps=[out_mm.get((g - 1, NRBK - 1))], inc="dx3g%d" % gb)
            for ei in range(9):
                e = ei - 1
                for r in range(NR):
                    rb = r % 2
                    rk = (g, ei * NR + r)
                    rkm2 = (g, ei * NR + r - 2) if ei * NR + r >= 2 else (g - 1, NRBK - 2 + ei * NR + r)
                    if ei == 0:
                        src1 = ws1[:, r * 1024:(r + 1) * 1024]
                        src2 = ws2[r * 1024:(r + 1) * 1024, :]
                    else:
                        src1 = we1[e][:, r * 1024:(r + 1) * 1024]
                        src2 = we2[e][r * 1024:(r + 1) * 1024, :]
                    w1tok = P.dma("gpsimd", w1b[:, rb, :, :],
                                  src1.rearrange("(a p) c -> p a c", p=128),
                                  deps=[out_mm.get(rkm2)], inc="dw1g%d" % rb)
                    w2tok = P.dma("gpsimd", w2b[:, rb, :, :],
                                  src2.rearrange("(i p) c -> p i c", p=128),
                                  deps=[out_mm.get(rkm2)], inc="dw2g%d" % rb)
                    if r == 0:
                        b2src = bs2 if ei == 0 else be2[e]
                        tb2 = P.dma("gpsimd", b2r[0:1, (ei % 2) * D:(ei % 2 + 1) * D],
                                    b2src, deps=[out_mm.get(rkm2)], inc="db2g%d" % (ei % 2))
                    for it in range(IPR):
                        itg = r * IPR + it
                        for half in range(GT // 512):
                            pp = (it * (GT // 512) + half) % 2
                            psx = psC[:, pp * 512:pp * 512 + 512]
                            mmt = None
                            for mt in range(8):
                                mmt = P.op("tensor",
                                           (lambda o=psx, w=w1b[:, rb, mt, it * 128:(it + 1) * 128],
                                            xx=xng[:, mt, half * 512:(half + 1) * 512],
                                            s0=(mt == 0), s1=(mt == 7):
                                            lambda E: E.matmul(o, w, xx, start=s0, stop=s1))(),
                                           deps=[xn_ld[g], w1tok, psC_tok.get(pp)],
                                           inc=("t3g" if mt == 7 else None))
                            bcol = b1c[:, ei * 32 + itg:ei * 32 + itg + 1]
                            psC_tok[pp] = P.op(
                                "scalar",
                                (lambda o=hbuf[:, rb, it, half * 512:(half + 1) * 512],
                                 s=psx, b=bcol:
                                 lambda E: E.activation(o, s, AF.Silu, bias=b))(),
                                deps=[mmt, tb1], inc="s3g")
                    h_done = psC_tok[(IPR * (GT // 512) - 1) % 2]
                    last_mm = None
                    for ts in range(NTS):
                        pq = ts % 2
                        psx = (psA if pq == 0 else psB)
                        mm0 = None
                        if r == 0:
                            for cf in range(2):
                                mm0 = P.op("tensor",
                                           (lambda o=psx[:, cf * 512:(cf + 1) * 512], w=onesb[0:1, :],
                                            xx=b2r[0:1, (ei % 2) * D + cf * 512:(ei % 2) * D + (cf + 1) * 512]:
                                            lambda E: E.matmul(o, w, xx, start=True, stop=False))(),
                                           deps=[h_done, w2tok, tb2, pAB_tok.get(pq)], inc=None)
                        mmt = None
                        for it in range(IPR):
                            for cf in range(2):
                                s0 = (r > 0 and it == 0)
                                mmt = P.op("tensor",
                                           (lambda o=psx[:, cf * 512:(cf + 1) * 512],
                                            w=hbuf[:, rb, it, ts * 128:(ts + 1) * 128],
                                            xx=w2b[:, rb, it, cf * 512:(cf + 1) * 512],
                                            s0=s0, s1=(it == IPR - 1):
                                            lambda E: E.matmul(o, w, xx, start=s0, stop=s1))(),
                                           deps=[h_done, w2tok, pAB_tok.get(pq)],
                                           inc=("t3g" if (it == IPR - 1 and cf == 1) else None))
                        tbg = g * NTS + ts
                        wcol = w_sb.ap()[:, tbg * 8 + e:tbg * 8 + e + 1]
                        if ei == 0 and r == 0:
                            tok = P.op("vector",
                                       (lambda o=acc[:, ts, :], s=psx:
                                        lambda E: E.tensor_copy(o, s))(),
                                       deps=[mmt, acc_free.get((g - 1, ts))], inc="v3g")
                        elif ei == 0:
                            tok = P.op("vector",
                                       (lambda o=acc[:, ts, :], s=psx, a=acc[:, ts, :]:
                                        lambda E: E.tensor_add(o, s, a))(),
                                       deps=[mmt, acc_tok.get(ts)], inc="v3g")
                        else:
                            tok = P.op("vector",
                                       (lambda o=acc[:, ts, :], s=psx, w=wcol, a=acc[:, ts, :]:
                                        lambda E: E.scalar_tensor_tensor(o, s, w, a, OP.mult, OP.add))(),
                                       deps=[mmt, acc_tok.get(ts)], inc="v3g")
                        acc_tok[ts] = tok
                        pAB_tok[pq] = tok
                        last_mm = mmt
                    out_mm[rk] = last_mm
            for ts in range(NTS):
                tbg = g * NTS + ts
                tsb = ts % 2
                pslot = (g, ts - 2) if ts >= 2 else (g - 1, NTS - 2 + ts)
                lx = P.dma("sync", xs[:, tsb, 0, :], x[tbg * 128:(tbg + 1) * 128, :],
                           deps=[ep_tok.get(("ta",) + pslot), st_dma.get(pslot)],
                           inc="dxs%d" % tsb)
                ls = P.dma("sync", xs[:, tsb, 1, :], s_hbm[tbg * 128:(tbg + 1) * 128, :],
                           deps=[st_dma.get(pslot)], inc="dxs%d" % tsb)
                tg_ = P.op("scalar", (lambda o=gps[:, tsb, :], s=acc[:, ts, :]:
                                      lambda E: E.activation(o, s, AF.Silu))(),
                           deps=[acc_tok[ts], ep_tok.get(("ta",) + pslot)], inc="s3g")
                acc_free[(g, ts)] = tg_
                tm = P.op("vector", (lambda o=gps[:, tsb, :], a=gps[:, tsb, :], b=xs[:, tsb, 1, :]:
                                     lambda E: E.tensor_mul(o, a, b))(),
                          deps=[tg_, ls], inc="v3g")
                ep_tok[("tm", g, ts)] = tm
                ta = P.op("vector", (lambda o=xs[:, tsb, 1, :], a=xs[:, tsb, 0, :], b=gps[:, tsb, :]:
                                     lambda E: E.tensor_add(o, a, b))(),
                          deps=[tm, lx], inc="v3g")
                ep_tok[("ta", g, ts)] = ta
                st_dma[(g, ts)] = P.dma("gpsimd", x1_hbm[tbg * 128:(tbg + 1) * 128, :],
                                        xs[:, tsb, 1, :], deps=[ta], inc="dx1%d" % tsb)
    P.barrier()


def p4(B, stack):
    B.phase_bufs(stack, {
        "nx_xbuf": ((128, 2 * D), F32),
        "sq_scr": ((128, 2048), F32),
        "nx_f32": ((128, 2 * D), F32),
        "nx_st": ((128, 8), F32),
        "nx_tm": ((128, B.NTB * D), BF16),
        "nx_stage": ((128, 2 * B.T), BF16),
    })
    x1 = B.dram["x1_hbm"]
    xn1_cm = B.dscratch("xn1_cm", (D, B.T), BF16)
    xn1_tm = B.dscratch("xn1_tm", (B.T, D), F32)
    emit_ln_to_cm(B, x1, xn1_cm, "p4", kind="ln", eps=1e-5, also_tm_dram=xn1_tm)


def pd(B, stack):
    import contextlib
    P, T, NTB = B.P, B.T, B.NTB
    DI, NHD, HD, DS, CD = 2048, 32, 64, 128, 2304
    NFT = 18
    Q = min(1024, T)
    NQ = T // Q
    NCH = Q // 128
    xn1_cm = B.dram["xn1_cm"]
    xn1_tm = B.dram["xn1_tm"]
    s1_hbm = B.dscratch("s1_hbm", (T, D), F32)
    z_hbm = B.dscratch("z_hbm", (T, DI), BF16)
    ipT = B.din("w_ipT", (D, 4384), F32)
    convw = B.din("w_convw", (128, NFT * 4), F32)
    convb = B.din("w_convb", (128, NFT), F32)
    dtb = B.din("w_dtb", (1, NHD), F32)
    alog = B.din("w_Alog", (1, NHD), F32)
    dh = B.din("w_Dh", (1, NHD), F32)
    normw = B.din("w_normw", (128, 16), F32)
    opT = B.din("w_opT", (DI, D), F32)

    psA, psB, psC, psT = B.psA.ap(), B.psB.ap(), B.psC.ap(), B.psT.ap()
    psD = psC
    onesf = B.sb["ones_f32"].ap()
    triu = B.sb["triu"].ap()
    row127 = B.sb["row127"].ap()
    identb = B.sb["ident_bf"].ap()

    B.phase_bufs(stack, {
        "pd_xn1q": ((128, 8 * Q), BF16),
        "pd_xs": ((128, 8 * DI if Q >= 1024 else NCH * DI), BF16),
        "pd_Bc": ((128, Q), BF16),
        "pd_Cc": ((128, Q), BF16),
        "pd_dt": ((128, 8 * NHD), F32),
        "pd_x1t": ((128, D), F32),
        "pd_hT": ((128, DI), F32),
        "pd_hTb": ((128, DI), BF16),
        "pd_y": ((128, DI), F32),
        "pd_yn": ((128, DI), BF16),
        "pd_ynT": ((128, DI), BF16),
        "pd_sm": ((128, 224), F32),
        "pd_W2": ((128, 2 * 96), F32),
        "pd_W96": ((96, 2 * 128), F32),
        "pd_alpT": ((32, 2 * 128), BF16),
        "pd_clflA": ((1, 2 * 2048), F32),
        "pd_clflB": ((1, 2 * 2048), F32),
        "pd_nclT": ((4, 2 * 1024), F32),
        "pd_Dscr": ((128, 512), F32),
        "pd_shared": ((128, 9216), BF16),
        "pd_Yp": ((128, 2 * DI), BF16),
        "pd_Btm": ((128, 2 * 128), BF16),
        "pd_ew": ((128, 2 * 64), F32),
        "pd_halo": ((128, NFT * 4), F32),
        "pd_s1": ((128, D), F32),
        "pd_opw": ((128, 16 * D), BF16),
        "pd_rows": ((1, 64), F32),
        "pd_ind4": ((4, 512), F32),
        "pd_ind64": ((32, 2048), BF16),
        "pd_dexp": ((128, 2048), BF16),
    })
    xn1q = B.sb["pd_xn1q"].ap().rearrange("p (a t) -> p a t", a=8)
    xsq = B.sb["pd_xs"].ap()
    Bc = B.sb["pd_Bc"].ap()
    Cc = B.sb["pd_Cc"].ap()
    dtq = B.sb["pd_dt"].ap()
    zq = B.sb["pd_yn"].ap()
    hT = B.sb["pd_hT"].ap()
    hTb = B.sb["pd_hTb"].ap()
    ynb = B.sb["pd_yn"].ap()
    halo = B.sb["pd_halo"].ap()
    opw = B.sb["pd_opw"].ap().rearrange("p (c m) -> p c m", c=16)
    rows = B.sb["pd_rows"].ap()
    sm = B.sb["pd_sm"].ap()
    S_adt = sm[:, 160:192]
    S_alp = sm[:, 192:224]

    t_ab = P.dma("sync", rows[0:1, 0:NHD], alog, inc="dpA")
    tok = P.op("tensor", (lambda o=psD[:, 0:NHD], w=onesf, x=rows[0:1, 0:NHD]:
                          lambda E: E.matmul(o, w, x, start=True, stop=True))(),
               deps=[t_ab], inc="tpd")
    abc = stack.enter_context(B.nc.sbuf_tensor("pd_abc", [128, 3 * NHD], F32)).ap()
    t_a1 = P.op("scalar", (lambda o=abc[:, 0:NHD], s=psD[:, 0:NHD]:
                           lambda E: E.activation(o, s, AF.Exp))(), deps=[tok], inc="spd")
    t_a2 = P.op("vector", (lambda o=abc[:, 0:NHD], s=abc[:, 0:NHD]:
                           lambda E: E.tensor_scalar(o, s, -1.0, None, OP.mult))(),
                deps=[t_a1], inc="vpd")
    t_db = P.dma("sync", rows[0:1, 0:NHD], dh, deps=[tok], inc="dpD")
    tok2 = P.op("tensor", (lambda o=psD[:, 0:NHD], w=onesf, x=rows[0:1, 0:NHD]:
                           lambda E: E.matmul(o, w, x, start=True, stop=True))(),
                deps=[t_db, t_a1], inc="tpd")
    t_d1 = P.op("vector", (lambda o=abc[:, NHD:2 * NHD], s=psD[:, 0:NHD]:
                           lambda E: E.tensor_copy(o, s))(), deps=[tok2], inc="vpd")
    t_dtb = P.dma("sync", rows[0:1, 32:32 + NHD], dtb, deps=[], inc="dpT")
    t_nw = P.dma("sync", abc[:, 2 * NHD:2 * NHD + 16], normw, inc="dpN")
    t_ow = P.dma("gpsimd", opw, opT.rearrange("(c p) m -> p c m", p=128), inc="dpd1")
    t_owf = None
    for ct in range(16):
        t_owf = P.op("vector",
                     (lambda o=opw[:, ct, :], s=opw[:, ct, :], g=abc[:, 2 * NHD + ct:2 * NHD + ct + 1]:
                      lambda E: E.tensor_scalar(o, s, g, None, OP.mult))(),
                     deps=[t_ow, t_nw], inc="vpd")
    B.pd_abc = abc
    B.pd_cw = stack.enter_context(B.nc.sbuf_tensor("pd_cw", [128, NFT * 5], F32)).ap()
    P.dma("sync", B.pd_cw[:, 0:NFT * 4], convw, inc="dcw")
    B.pd_cw_tok = P.dma("sync", B.pd_cw[:, NFT * 4:NFT * 5], convb, inc="dcw")
    ind4_src = B.din("c_ind4", (4, 512), F32)
    ind64_src = B.din("c_ind64", (32, 2048), BF16)
    dexp_src = B.din("w_Dexp128", (128, 2048), BF16)
    P.dma("sync", B.sb["pd_ind4"].ap(), ind4_src, inc="dcw")
    P.dma("sync", B.sb["pd_ind64"].ap(), ind64_src, inc="dcw")
    B.pd_ind_tok = P.dma("sync", B.sb["pd_dexp"].ap(), dexp_src, inc="dcw")
    B.pd_wsc = B.dscratch("pd_wsc", (2, 64, 128), F32)
    B.pd_abc_tok = t_d1
    B.pd_opw_tok = t_owf
    B.pd_tok = {}
    B.pd_psT_state = [None, None]
    B.pd_halo_tok = {}

    last_q_tok = t_a2
    zst_prev = {}
    B.pd_zst = zst_prev
    for q in range(NQ):
        with contextlib.ExitStack() as sq:
            wz = B.sb["pd_shared"].ap()[:, 0:8448].rearrange("p (a c) -> p a c", a=8)
            tq = P.dma("sync", xn1q, xn1_cm.rearrange("(a p) t -> p a t", p=128)[:, :, q * Q:(q + 1) * Q],
                       deps=[last_q_tok], inc="dxq%d" % (q % 2))
            ztok = {}
            zst_prev = B.pd_zst
            mmz_last = None
            for zh in range(2):
                if zh == 0:
                    wsrc = ipT[:, 0:1024]
                else:
                    wsrc = ipT[:, 1024:2048]
                twz = P.dma("gpsimd", wz[:, :, 0:1024],
                            wsrc.rearrange("(a p) c -> p a c", p=128),
                            deps=[last_q_tok, mmz_last], inc="dwz%d" % zh)
                if zh == 0:
                    twd = P.dma("gpsimd", wz[:, :, 1024:1024 + NHD],
                                ipT[:, 4352:4384].rearrange("(a p) c -> p a c", p=128),
                                deps=[last_q_tok], inc="dwzd")
                for tbq in range(NCH):
                    tb = q * NCH + tbq
                    mm_z = None
                    for mt in range(8):
                        for cf in range(2):
                            psx = psA[:, cf * 512:(cf + 1) * 512]
                            mm_z = P.op("tensor",
                                        (lambda o=psx,
                                         w=xn1q[:, mt, tbq * 128:(tbq + 1) * 128],
                                         x=wz[:, mt, cf * 512:cf * 512 + 512],
                                         s0=(mt == 0), s1=(mt == 7):
                                         lambda E: E.matmul(o, w, x, start=s0, stop=s1))(),
                                        deps=[tq, twz, ztok.get((zh, tbq - 1)),
                                              ztok.get((zh - 1, NCH - 1))],
                                        inc=("tpd" if (mt == 7 and cf == 1) else None))
                    if zh == 0:
                        mm_d = None
                        for mt in range(8):
                            mm_d = P.op("tensor",
                                        (lambda o=psC[:, 0:NHD],
                                         w=xn1q[:, mt, tbq * 128:(tbq + 1) * 128],
                                         x=wz[:, mt, 1024:1024 + NHD],
                                         s0=(mt == 0), s1=False:
                                         lambda E: E.matmul(o, w, x, start=s0, stop=s1))(),
                                        deps=[tq, twd, ztok.get(("d", tbq - 1)), B.pd_abc_tok],
                                        inc=None)
                        mm_d = P.op("tensor",
                                    (lambda o=psC[:, 0:NHD], w=onesf, x=rows[0:1, 32:32 + NHD]:
                                     lambda E: E.matmul(o, w, x, start=False, stop=True))(),
                                    deps=[t_dtb], inc="tpd")
                    t_z1 = P.op("scalar", (lambda o=zq[:, 0:1024], s=psA:
                                           lambda E: E.activation(o, s, AF.Silu))(),
                                deps=[mm_z, zst_prev.get(0), zst_prev.get(1)], inc="spd")
                    ztok[(zh, tbq)] = t_z1
                    mmz_last = mm_z
                    zst_prev[tbq % 2] = P.dma(
                        "gpsimd", z_hbm[tb * 128:(tb + 1) * 128, zh * 1024:(zh + 1) * 1024],
                        zq[:, 0:1024], deps=[t_z1], inc="dzs%d" % (tbq % 2))
                    if zh == 0:
                        c0 = tbq * NHD
                        t_s1 = P.op("scalar", (lambda o=S_adt, s=psC[:, 0:NHD]:
                                               lambda E: E.activation(o, s, AF.Abs))(),
                                    deps=[mm_d, ztok.get(("sp", tbq - 1))], inc="spd")
                        t_s2 = P.op("scalar", (lambda o=S_adt, s=S_adt:
                                               lambda E: E.activation(o, s, AF.Exp, scale=-1.0))(),
                                    deps=[t_s1], inc="spd")
                        t_s3 = P.op("scalar", (lambda o=S_adt, s=S_adt:
                                               lambda E: E.activation(o, s, AF.Ln, bias=1.0))(),
                                    deps=[t_s2], inc="spd")
                        t_s4 = P.op("vector", (lambda o=S_alp, s=psC[:, 0:NHD]:
                                               lambda E: E.tensor_scalar_max(o, s, 0.0))(),
                                    deps=[mm_d, ztok.get(("sp", tbq - 1))], inc="vpd")
                        t_s5 = P.op("vector", (lambda o=dtq[:, c0:c0 + NHD], a=S_adt, b=S_alp:
                                               lambda E: E.tensor_add(o, a, b))(),
                                    deps=[t_s3, t_s4], inc="vpd")
                        ztok[("d", tbq)] = t_s5
                        ztok[("sp", tbq)] = t_s5
            pdA_done = [ztok[(1, NCH - 1)], ztok[("d", NCH - 1)]]
            B.pd_dt_done = ztok[("d", NCH - 1)]
        with contextlib.ExitStack() as sq:
            cv = B.sb["pd_Yp"].ap()[:, 0:Q + 4]
            cv2 = B.sb["pd_y"].ap()[:, 0:Q]
            wx = B.sb["pd_shared"].ap()[:, 0:2048].rearrange("p (a b c) -> p a b c", a=8, b=2)
            cwsb = B.pd_cw
            ft_tok = {}
            tr_state = None
            for ft in range(NFT):
                twx = P.dma("gpsimd", wx[:, :, ft % 2, :],
                            ipT[:, DI + ft * 128:DI + (ft + 1) * 128].rearrange("(a p) c -> p a c", p=128),
                            deps=(pdA_done + [last_q_tok] if ft < 2 else
                                  [ft_tok[("mm", ft - 2, Q // 512 - 1)]]),
                            inc="dwx%d" % (ft % 2))
                pp = ft % 2
                psx = psC[:, pp * 512:(pp + 1) * 512]
                evt = {}
                for cf in range(Q // 512):
                    mmx = None
                    for mt in range(8):
                        mmx = P.op("tensor",
                                   (lambda o=psx,
                                    w=wx[:, mt, ft % 2, :],
                                    x=xn1q[:, mt, cf * 512:cf * 512 + 512],
                                    s0=(mt == 0), s1=(mt == 7):
                                    lambda E: E.matmul(o, w, x, start=s0, stop=s1))(),
                                   deps=[twx, tq, ft_tok.get((ft - 2, cf)),
                                         ft_tok.get((ft, cf - 1))],
                                   inc=("tpd" if mt == 7 else None))
                    evt[cf] = P.op("vector",
                                   (lambda o=cv[:, 3 + cf * 512:3 + cf * 512 + 512], s=psx:
                                    lambda E: E.tensor_copy(o, s))(),
                                   deps=[mmx, ft_tok.get(("cv", ft - 1))], inc="vpd")
                    ft_tok[(ft, cf)] = evt[cf]
                    ft_tok[("mm", ft, cf)] = mmx
                if q == 0:
                    th = P.op("vector", (lambda o=cv[:, 0:3]: lambda E: E.memset(o, 0.0))(),
                              deps=[ft_tok.get(("cv", ft - 1))], inc="vpd")
                else:
                    th = P.op("vector", (lambda o=cv[:, 0:3], s=halo[:, ft * 3:ft * 3 + 3]:
                                         lambda E: E.tensor_copy(o, s))(),
                              deps=[ft_tok.get(("cv", ft - 1)), B.pd_halo_tok.get(ft)],
                              inc="vpd")
                tc = P.op("vector", (lambda o=cv2[:, 0:Q], s=cv[:, 0:Q], w=cwsb[:, ft * 4:ft * 4 + 1]:
                                     lambda E: E.tensor_scalar(o, s, w, None, OP.mult))(),
                          deps=[evt[Q // 512 - 1], th, B.pd_cw_tok, ft_tok.get(("c2", ft - 1)),
                                ft_tok.get(("tr", ft - 1)), B.pd_zst.get(0), B.pd_zst.get(1)],
                          inc="vpd")
                for kk in range(1, 4):
                    tc = P.op("vector",
                              (lambda o=cv2[:, 0:Q], s=cv[:, kk:kk + Q],
                               w=cwsb[:, ft * 4 + kk:ft * 4 + kk + 1], a=cv2[:, 0:Q]:
                               lambda E: E.scalar_tensor_tensor(o, s, w, a, OP.mult, OP.add))(),
                              deps=[tc], inc="vpd")
                ft_tok[("c2", ft)] = tc
                B.pd_halo_tok[ft] = P.op("vector",
                                         (lambda o=halo[:, ft * 3:ft * 3 + 3], s=cv[:, Q:Q + 3]:
                                          lambda E: E.tensor_copy(o, s))(),
                                         deps=[evt[Q // 512 - 1], th], inc="vpd")
                if ft < 16:
                    sil = P.op("scalar",
                               (lambda o=cv2[:, 0:Q], s=cv2[:, 0:Q],
                                b=cwsb[:, NFT * 4 + ft:NFT * 4 + ft + 1]:
                                lambda E: E.activation(o, s, AF.Silu, bias=b))(),
                               deps=[tc], inc="spd")
                    cvb = B.sb["pd_yn"].ap()
                    tcb = P.op("vector", (lambda o=cvb[:, 0:Q], s=cv2[:, 0:Q]:
                                          lambda E: E.tensor_copy(o, s))(),
                               deps=[sil, ft_tok.get(("tr", ft - 1))], inc="vpd")
                    if tr_state is None:
                        tr_state = [None, None]
                    lastev = None
                    for g0 in range(0, NCH, 4):
                        half = (g0 // 4) % 2
                        cnt = min(4, NCH - g0)
                        ttoks = []
                        for j in range(cnt):
                            tbq2 = g0 + j
                            dstp = psT[:, half * 1024 + j * 128: half * 1024 + (j + 1) * 128]
                            ttoks.append(P.op("tensor",
                                              (lambda dstp=dstp, s2=cvb[:, tbq2 * 128:(tbq2 + 1) * 128]:
                                               lambda E: E.transpose(dstp, s2, identb))(),
                                              deps=[tcb, tr_state[half]], inc="tpd"))
                        srcp = psT[:, half * 1024: half * 1024 + cnt * 128].rearrange(
                            "p (b c) -> p b c", b=cnt)
                        dsta = xsq.rearrange("p (b c) -> p b c", b=NCH)[
                            :, g0:g0 + cnt, ft * 128:(ft + 1) * 128]
                        tr_state[half] = P.op("vector",
                                              (lambda dsta=dsta, srcp=srcp:
                                               lambda E: E.tensor_copy(dsta, srcp))(),
                                              deps=[ttoks[-1], ft_tok.get(("trw", ft - 2))],
                                              inc="vpd")
                        lastev = tr_state[half]
                    ft_tok[("trw", ft)] = lastev
                    ft_tok[("tr", ft)] = lastev
                else:
                    dst = Bc if ft == 16 else Cc
                    sil = P.op("scalar",
                               (lambda o=dst[:, 0:Q], s=cv2[:, 0:Q],
                                b=cwsb[:, NFT * 4 + ft:NFT * 4 + ft + 1]:
                                lambda E: E.activation(o, s, AF.Silu, bias=b))(),
                               deps=[tc], inc="spd")
                    ft_tok[("tr", ft)] = sil
                ft_tok[("cv", ft)] = B.pd_halo_tok[ft]
            pdB_join = P.op("sync", lambda E: E.nop(),
                            deps=[ft_tok[("tr", NFT - 1)], ft_tok[("c2", NFT - 1)],
                                  ft_tok[("trw", 15)], ft_tok[("trw", 14)],
                                  ft_tok[("tr", 16)], B.pd_halo_tok[NFT - 1],
                                  B.pd_halo_tok[NFT - 2]], inc="qpd")
        last_q_tok = pd_chunks(B, q, Q, NCH, pdB_join, xn1_tm, s1_hbm, z_hbm)
    P.barrier()


def pd_chunks(B, q, Q, NCH, pdB_join, xn1_tm, s1_hbm, z_hbm):
    P = B.P
    DI, NHD = 2048, 32
    psA, psB, psC, psT = B.psA.ap(), B.psB.ap(), B.psC.ap(), B.psT.ap()
    onesf = B.sb["ones_f32"].ap()
    triu = B.sb["triu"].ap()
    row127 = B.sb["row127"].ap()
    identb = B.sb["ident_bf"].ap()
    identf = B.sb["ident_f32"].ap()
    xsq = B.sb["pd_xs"].ap()
    Bc = B.sb["pd_Bc"].ap()
    Cc = B.sb["pd_Cc"].ap()
    dtq = B.sb["pd_dt"].ap()
    zq = B.sb["pd_yn"].ap()
    x1t = B.sb["pd_x1t"].ap()
    hT = B.sb["pd_hT"].ap()
    hTb = B.sb["pd_hTb"].ap()
    ybuf = B.sb["pd_y"].ap()
    ynb = B.sb["pd_yn"].ap()
    ynT = B.sb["pd_ynT"].ap()
    sm = B.sb["pd_sm"].ap()
    W2 = B.sb["pd_W2"].ap()
    W96 = B.sb["pd_W96"].ap()
    alpT = B.sb["pd_alpT"].ap()
    clflA = B.sb["pd_clflA"].ap()
    clflB = B.sb["pd_clflB"].ap()
    nclT = B.sb["pd_nclT"].ap()
    Dscr = B.sb["pd_Dscr"].ap()
    shared = B.sb["pd_shared"].ap()
    Msb = shared[:, 0:4096]
    G4 = shared[:, 4096:5120]
    Up = shared[:, 5120:9216]
    Yp = B.sb["pd_Yp"].ap()
    BtmB = B.sb["pd_Btm"].ap()
    ew = B.sb["pd_ew"].ap()
    s1b = B.sb["pd_s1"].ap()
    opw = B.sb["pd_opw"].ap().rearrange("p (c m) -> p c m", c=16)
    ind4 = B.sb["pd_ind4"].ap()
    ind64 = B.sb["pd_ind64"].ap()
    dexp = B.sb["pd_dexp"].ap()
    abc = B.pd_abc
    A_bc = abc[:, 0:NHD]
    wsc = B.pd_wsc
    pstate = B.pd_psT_state

    tk = B.pd_tok
    if q == 0:
        tk["h0"] = P.op("vector", (lambda o=hT: lambda E: E.memset(o, 0.0))(), inc="vpd")
        tk[("hbcast", -1)] = P.op("vector", (lambda o=hTb: lambda E: E.memset(o, 0.0))(),
                                  deps=[tk["h0"]], inc="vpd")

    def prep(tbq):
        tb = q * NCH + tbq
        sl = tb % 2
        c0 = tbq * NHD
        fj = pdB_join
        adt = sm[:, sl * 32: sl * 32 + 32]
        tmp = sm[:, 64 + sl * 32: 64 + sl * 32 + 32]
        W2s = W2[:, sl * 96:(sl + 1) * 96]
        W96s = W96[0:96, sl * 128:(sl + 1) * 128]
        alpTs = alpT[0:32, sl * 128:(sl + 1) * 128]
        G4s = G4[:, sl * 512:(sl + 1) * 512]
        Btms = BtmB[:, sl * 128:(sl + 1) * 128]
        Ups = Up[:, sl * DI:(sl + 1) * DI]
        Yps = Yp[:, sl * DI:(sl + 1) * DI]
        ecls = ew[:, sl * 64: sl * 64 + 32]
        wends = ew[:, sl * 64 + 32: sl * 64 + 64]
        xsc = xsq[:, tbq * DI:(tbq + 1) * DI]
        Bcc = Bc[:, tbq * 128:(tbq + 1) * 128]
        Ccc = Cc[:, tbq * 128:(tbq + 1) * 128]
        t_adt = P.op("vector", (lambda o=adt, a=dtq[:, c0:c0 + NHD], b=A_bc:
                                lambda E: E.tensor_mul(o, a, b))(),
                     deps=[fj, B.pd_dt_done, B.pd_abc_tok, tk.get(("adtfree", tb - 2))],
                     inc="vpd")
        t_cum = P.op("tensor", (lambda o=psB[:, 512:544], w=triu, x=adt:
                                lambda E: E.matmul(o, w, x, start=True, stop=True))(),
                     deps=[t_adt, tk.get(("b3prep", tb - 1)), tk.get(("b3main", tb - 1))],
                     inc="tpd")
        tk[("adtfree", tb)] = t_cum
        t_cl = P.op("vector", (lambda o=W2s[:, 0:32], s=psB[:, 512:544]:
                               lambda E: E.tensor_copy(o, s))(),
                    deps=[t_cum, tk.get(("W2free", tb - 2))], inc="vpd")
        t_wm = P.op("tensor", (lambda o=psB[:, 544:576], w=row127, x=W2s[:, 0:32]:
                               lambda E: E.matmul(o, w, x, start=True, stop=True))(),
                    deps=[t_cl], inc="tpd")
        t_wtm = P.op("vector", (lambda o=tmp, s=psB[:, 544:576]:
                                lambda E: E.tensor_copy(o, s))(), deps=[t_wm], inc="vpd")
        tk[("b3prep", tb)] = t_wtm
        t_wend = P.op("scalar", (lambda o=wends, s=tmp: lambda E: E.activation(o, s, AF.Exp))(),
                      deps=[t_wtm, tk.get(("ewfree", tb - 2))], inc="spd")
        tk[("wend", tb)] = t_wend
        t_ecl = P.op("scalar", (lambda o=ecls, s=W2s[:, 0:32]:
                                lambda E: E.activation(o, s, AF.Exp))(),
                     deps=[t_cl, tk.get(("ewfree", tb - 2))], inc="spd")
        tk[("ecl", tb)] = t_ecl
        t_sub = P.op("vector", (lambda o=tmp, a=tmp, b=W2s[:, 0:32]:
                                lambda E: E.tensor_sub(o, a, b))(),
                     deps=[t_wend, t_cl], inc="vpd")
        t_e2 = P.op("scalar", (lambda o=tmp, s=tmp: lambda E: E.activation(o, s, AF.Exp))(),
                    deps=[t_sub], inc="spd")
        t_alp = P.op("vector", (lambda o=W2s[:, 64:96], a=tmp, b=dtq[:, c0:c0 + NHD]:
                                lambda E: E.tensor_mul(o, a, b))(),
                     deps=[t_e2], inc="vpd")
        t_ln = P.op("scalar", (lambda o=W2s[:, 32:64], s=dtq[:, c0:c0 + NHD]:
                               lambda E: E.activation(o, s, AF.Ln))(),
                    deps=[B.pd_dt_done, tk.get(("W2free", tb - 2))], inc="spd")
        t_ls = P.op("vector", (lambda o=W2s[:, 32:64], a=W2s[:, 32:64], b=W2s[:, 0:32]:
                               lambda E: E.tensor_sub(o, a, b))(),
                    deps=[t_ln, t_cl], inc="vpd")
        t_w2t = P.op("tensor", (lambda o=psB[0:96, 0:128], w=W2s, x=identf:
                                lambda E: E.matmul(o, w, x, start=True, stop=True))(),
                     deps=[t_alp, t_ls, tk.get(("b2free", tb - 1)), tk.get(("b2main", tb - 1))],
                     inc="tpd")
        tk[("W2free", tb)] = t_w2t
        t_w96 = P.op("vector", (lambda o=W96s, s=psB[0:96, 0:128]:
                                lambda E: E.tensor_copy(o, s))(),
                     deps=[t_w2t, tk.get(("w96free", tb - 2))], inc="vpd")
        tk[("b2free", tb)] = t_w96
        t_alpT = P.op("vector", (lambda o=alpTs, s=W96s[64:96, 0:128]:
                                 lambda E: E.tensor_copy(o, s))(),
                      deps=[t_w96], inc="vpd")
        d_out = P.dma("gpsimd", wsc[sl], W96s[0:64, 0:128], deps=[t_w96],
                      inc="dwsc%d" % sl)
        tk[("w96free", tb)] = d_out
        flat0 = wsc[sl][0:16, :].rearrange("h t -> (h t)").rearrange("(o c) -> o c", o=1)
        flat1 = wsc[sl][16:32, :].rearrange("h t -> (h t)").rearrange("(o c) -> o c", o=1)
        d_cl0 = P.dma("gpsimd", clflA[0:1, sl * 2048:(sl + 1) * 2048], flat0,
                      deps=[d_out, tk.get(("clflfree", tb - 2))], inc="dclf%d" % sl)
        d_cl1 = P.dma("gpsimd", clflB[0:1, sl * 2048:(sl + 1) * 2048], flat1,
                      deps=[d_out], inc="dclf%d" % sl)
        d_ncl = P.dma("gpsimd",
                      nclT[0:4, sl * 1024:(sl + 1) * 1024].rearrange("q (g t) -> q g t", g=8),
                      wsc[sl][32:64, :].rearrange("(g q) t -> q g t", q=4),
                      deps=[d_out, tk.get(("nclfree", tb - 2))], inc="dclf%d" % sl)
        tk[("dcl", tb)] = d_cl1
        tk[("dcl0", tb)] = d_cl0
        tk[("dncl", tb)] = d_ncl
        t_up = None
        for r in range(2):
            mm = None
            for half in range(2):
                dep = [t_alpT, B.pd_ind_tok]
                if r == 0 and half == 0:
                    dep += [tk.get(("gmev", tb - 1)), tk.get(("hupd2", tb - 1)),
                            tk.get(("ypready", tb - 1))]
                if r == 1 and half == 0:
                    dep += [t_up]
                mm = P.op("tensor",
                          (lambda o=psC[:, half * 512:(half + 1) * 512], w=alpTs,
                           x=ind64[:, r * 1024 + half * 512: r * 1024 + (half + 1) * 512]:
                           lambda E: E.matmul(o, w, x, start=True, stop=True))(),
                          deps=dep, inc=("tpd" if half == 1 else None))
            t_up = P.op("vector",
                        (lambda o=Ups[:, r * 1024:(r + 1) * 1024], a=psC[:, 0:1024],
                         b=xsc[:, r * 1024:(r + 1) * 1024]:
                         lambda E: E.tensor_mul(o, a, b))(),
                        deps=[mm, tk.get(("upfree", tb - 2))], inc="vpd")
        tk[("up", tb)] = t_up
        t_gm = P.op("tensor", (lambda o=psC[:, 0:128], w=Bcc, x=Ccc:
                               lambda E: E.matmul(o, w, x, start=True, stop=True))(),
                    deps=[fj, t_up], inc="tpd")
        t_g = P.op("vector", (lambda o=G4s[:, 0:128], a=psC[:, 0:128], b=triu:
                              lambda E: E.tensor_mul(o, a, b))(),
                   deps=[t_gm, tk.get(("g4free", tb - 2))], inc="vpd")
        for k in range(1, 4):
            t_g = P.op("vector", (lambda o=G4s[:, k * 128:(k + 1) * 128], s=G4s[:, 0:128]:
                                  lambda E: E.tensor_copy(o, s))(),
                       deps=[t_g], inc="vpd")
        tk[("gmev", tb)] = t_g
        t_bt = P.op("tensor", (lambda o=psT[:, 0:128], w=Bcc, x=identb:
                               lambda E: E.transpose(o, w, x))(),
                    deps=[fj, pstate[0]], inc="tpd")
        t_bte = P.op("vector", (lambda o=Btms, s=psT[:, 0:128]:
                                lambda E: E.tensor_copy(o, s))(),
                     deps=[t_bt, tk.get(("btfree", tb - 2))], inc="vpd")
        tk[("btm", tb)] = t_bte
        pstate[0] = t_bte
        t_ydx = P.op("vector", (lambda o=Yps, a=dexp, b=xsc:
                                lambda E: E.tensor_mul(o, a, b))(),
                     deps=[fj, B.pd_ind_tok, tk.get(("ypfree", tb - 2))], inc="vpd")
        tk[("ydx", tb)] = t_ydx

    def mainA(tbq, blk_evs):
        tb = q * NCH + tbq
        sl = tb % 2
        Yps = Yp[:, sl * DI:(sl + 1) * DI]
        G4s = G4[:, sl * 512:(sl + 1) * 512]
        xsc = xsq[:, tbq * DI:(tbq + 1) * DI]
        t_y0 = None
        last_intra = None
        for g in range(8):
            bank = g % 2
            reg = psB[:, bank * 512:(bank + 1) * 512]
            clf = clflA if g < 4 else clflB
            coff = sl * 2048 + (g % 4) * 512
            dep0 = [tk[("dncl", tb)], tk.get(("clamp", tb, g - 2))]
            if g < 2:
                dep0 += blk_evs + [tk.get(("clamp", tb - 1, 6 + g))]
            P.op("tensor", (lambda o=reg, w=nclT[0:4, sl * 1024 + g * 128: sl * 1024 + (g + 1) * 128],
                            x=ind4: lambda E: E.matmul(o, w, x, start=True, stop=False))(),
                 deps=dep0, inc=None)
            m_a = P.op("tensor", (lambda o=reg, w=onesf, x=clf[0:1, coff:coff + 512]:
                                  lambda E: E.matmul(o, w, x, start=False, stop=True))(),
                       deps=[tk[("dcl", tb)], tk[("dcl0", tb)]], inc="tpd")
            t_cm = P.op("vector", (lambda o=Dscr[:, 0:512], s=reg:
                                   lambda E: E.tensor_scalar_min(o, s, 30.0))(),
                        deps=[m_a, tk.get(("mexp", tb, g - 1)),
                              tk.get(("mexp", tb - 1, 7)) if g == 0 else None], inc="vpd")
            tk[("clamp", tb, g)] = t_cm
            if g % 2 == 1:
                tk[("b3main", tb)] = t_cm
            else:
                tk[("b2main", tb)] = t_cm
            t_ex = P.op("scalar", (lambda o=Msb[:, g * 512:(g + 1) * 512],
                                   s=Dscr[:, 0:512]:
                                   lambda E: E.activation(o, s, AF.Exp))(),
                        deps=[t_cm, tk.get(("msbfree", tb - 1, g))], inc="spd")
            tk[("mexp", tb, g)] = t_ex
            t_mg = P.op("vector", (lambda o=Msb[:, g * 512:(g + 1) * 512],
                                   a=Msb[:, g * 512:(g + 1) * 512], b=G4s:
                                   lambda E: E.tensor_mul(o, a, b))(),
                        deps=[t_ex, tk[("gmev", tb)]], inc="vpd")
            for j in range(4):
                h = 4 * g + j
                dep = [t_mg]
                if h == 0:
                    dep += [tk[("gmev", tb)], tk.get(("gmev", tb + 1)),
                            tk.get(("hupd2", tb - 1)), tk.get(("yAe2", tb - 1))]
                if h == 16:
                    dep += [t_y0]
                last_intra = P.op("tensor",
                                  (lambda o=psC[:, (h % 16) * 64:((h % 16) + 1) * 64],
                                   w=Msb[:, h * 128:(h + 1) * 128],
                                   x=xsc[:, h * 64:(h + 1) * 64]:
                                   lambda E: E.matmul(o, w, x, start=True, stop=True))(),
                                  deps=dep, inc=("tpd" if j == 3 else None))
            tk[("msbfree", tb, g)] = last_intra
            if g == 3:
                t_y0 = P.op("vector", (lambda o=Yps[:, 0:1024], s=psC[:, 0:1024],
                                       a=Yps[:, 0:1024]:
                                       lambda E: E.tensor_add(o, s, a))(),
                            deps=[last_intra, tk[("ydx", tb)]], inc="vpd")
        t_y1 = P.op("vector", (lambda o=Yps[:, 1024:2048], s=psC[:, 0:1024],
                               a=Yps[:, 1024:2048]:
                               lambda E: E.tensor_add(o, s, a))(),
                    deps=[last_intra, tk[("ydx", tb)], t_y0], inc="vpd")
        tk[("ypready", tb)] = t_y1

    def phaseB(tbq, first_in_block):
        tb = q * NCH + tbq
        sl = tb % 2
        Yps = Yp[:, sl * DI:(sl + 1) * DI]
        Ups = Up[:, sl * DI:(sl + 1) * DI]
        Btms = BtmB[:, sl * 128:(sl + 1) * 128]
        ecls = ew[:, sl * 64: sl * 64 + 32]
        wends = ew[:, sl * 64 + 32: sl * 64 + 64]
        zqs = zq[:, 0:DI]
        x1ts = x1t[:, 0:1024]
        s1sl = s1b[:, 0:1024]
        rst = sm[:, 128 + sl * 8: 128 + sl * 8 + 4]
        zl = P.dma("sync", zqs, z_hbm[tb * 128:(tb + 1) * 128, :],
                   deps=[tk.get(("gate", tb - 1)), tk.get(("trdone", tb - 1)),
                         B.pd_zst.get(0), B.pd_zst.get(1)],
                   inc="dzl%d" % sl)
        xl = P.dma("sync", x1ts, xn1_tm[tb * 128:(tb + 1) * 128, :],
                   deps=[tk.get(("ts1", tb - 1))], inc="dxl%d" % sl)
        psc_free = ([tk.get(("ypready", tb)), tk.get(("ypready", tb + 1))]
                    if first_in_block else [tk.get(("hupd2", tb - 1))])
        t_yb = P.op("vector", (lambda o=ybuf, s=Yps: lambda E: E.tensor_copy(o, s))(),
                    deps=[tk[("ypready", tb)], tk.get(("tyn", tb - 1))], inc="vpd")
        yAlast = None
        for r in range(2):
            dep = [tk.get(("hbcast", tb - 1))] + (psc_free if r == 0 else [yAlast])
            mm = None
            for half in range(2):
                mm = P.op("tensor",
                          (lambda o=psC[:, half * 512:(half + 1) * 512],
                           w=Cc[:, tbq * 128:(tbq + 1) * 128],
                           x=hTb[:, r * 1024 + half * 512: r * 1024 + (half + 1) * 512]:
                           lambda E: E.matmul(o, w, x, start=True, stop=True))(),
                          deps=dep, inc=("tpd" if half == 1 else None))
            for k in range(16):
                h = r * 16 + k
                dep = [mm, tk[("ecl", tb)], t_yb]
                yAlast = P.op("vector",
                              (lambda o=ybuf[:, h * 64:(h + 1) * 64],
                               s=psC[:, k * 64:(k + 1) * 64],
                               w=ecls[:, h:h + 1], a=ybuf[:, h * 64:(h + 1) * 64]:
                               lambda E: E.scalar_tensor_tensor(o, s, w, a, OP.mult, OP.add))(),
                              deps=dep, inc="vpd")
            tk[("yAe%d" % (r + 1), tb)] = yAlast
        hulast = None
        for r in range(2):
            dep = [tk[("yAe2", tb)], tk[("up", tb)], tk[("btm", tb)]] if r == 0 else [hulast]
            mm = None
            for half in range(2):
                mm = P.op("tensor",
                          (lambda o=psC[:, half * 512:(half + 1) * 512], w=Btms,
                           x=Ups[:, r * 1024 + half * 512: r * 1024 + (half + 1) * 512]:
                           lambda E: E.matmul(o, w, x, start=True, stop=True))(),
                          deps=dep, inc=("tpd" if half == 1 else None))
            for k in range(16):
                h = r * 16 + k
                hulast = P.op("vector",
                              (lambda o=hT[:, h * 64:(h + 1) * 64],
                               a=hT[:, h * 64:(h + 1) * 64], w=wends[:, h:h + 1],
                               b=psC[:, k * 64:(k + 1) * 64]:
                               lambda E: E.scalar_tensor_tensor(o, a, w, b, OP.mult, OP.add))(),
                              deps=[mm, tk[("wend", tb)]], inc="vpd")
        tk[("hupd2", tb)] = hulast
        tk[("hbcast", tb)] = P.op("vector", (lambda o=hTb, s=hT: lambda E: E.tensor_copy(o, s))(),
                                  deps=[hulast], inc="vpd")
        tk[("ewfree", tb)] = tk[("hbcast", tb)]
        tg1 = P.op("vector", (lambda o=ybuf, a=ybuf, b=zqs: lambda E: E.tensor_mul(o, a, b))(),
                   deps=[tk[("yAe2", tb)], zl], inc="vpd")
        tk[("gate", tb)] = tg1
        tk[("ypfree", tb)] = t_yb
        tk[("upfree", tb)] = hulast
        tk[("btfree", tb)] = hulast
        tk[("g4free", tb)] = tk[("ypready", tb)]
        tk[("clflfree", tb)] = tk[("clamp", tb, 7)]
        tk[("nclfree", tb)] = tk[("clamp", tb, 7)]
        tsq = P.op("scalar", (lambda o=ynb, s=ybuf, z=rst[:, 0:1]:
                              lambda E: E.activation(o, s, AF.Square, accum_out=z))(),
                   deps=[tg1, tk.get(("trdone", tb - 1))], inc="spd")
        tr1 = P.op("scalar", (lambda o=rst[:, 1:2], s=rst[:, 0:1]:
                              lambda E: E.activation(o, s, AF.Ln, bias=1e-5, scale=1.0 / DI))(),
                   deps=[tsq], inc="spd")
        tr2 = P.op("scalar", (lambda o=rst[:, 2:3], s=rst[:, 1:2]:
                              lambda E: E.activation(o, s, AF.Exp, scale=-0.5))(),
                   deps=[tr1], inc="spd")
        tyn = P.op("vector", (lambda o=ynb, s=ybuf, w=rst[:, 2:3]:
                              lambda E: E.tensor_scalar(o, s, w, None, OP.mult))(),
                   deps=[tr2, tsq], inc="vpd")
        tk[("tyn", tb)] = tyn
        srcs = [ynb[:, ct * 128:(ct + 1) * 128] for ct in range(16)]
        sdeps = [[tyn] for ct in range(16)]
        lastev, _ = transpose_tiles(B, srcs, sdeps, ynT, 0, "pd",
                                    extra_evac_dep=tk.get(("opdone", tb - 1)),
                                    psT_state=pstate)
        tk[("trdone", tb)] = lastev
        opm = None
        for ct in range(16):
            for cf in range(2):
                opm = P.op("tensor",
                           (lambda o=psA[:, cf * 512:(cf + 1) * 512],
                            w=ynT[:, ct * 128:(ct + 1) * 128],
                            x=opw[:, ct, cf * 512:(cf + 1) * 512],
                            s0=(ct == 0), s1=(ct == 15):
                            lambda E: E.matmul(o, w, x, start=s0, stop=s1))(),
                           deps=[lastev, B.pd_opw_tok, tk.get(("ts1", tb - 1))],
                           inc=("tpd" if ct == 15 else None))
        tk[("opdone", tb)] = opm
        ts1 = P.op("vector", (lambda o=s1sl, s=psA[:, 0:1024], a=x1ts:
                              lambda E: E.tensor_add(o, s, a))(),
                   deps=[opm, xl, tk.get(("s1d", tb - 1))], inc="vpd")
        tk[("ts1", tb)] = ts1
        tk[("s1d", tb)] = P.dma("gpsimd", s1_hbm[tb * 128:(tb + 1) * 128, :], s1sl,
                                deps=[ts1], inc="ds1%d" % sl)

    for blk in range(NCH // 2):
        c0, c1 = blk * 2, blk * 2 + 1
        prep(c0)
        prep(c1)
        blk_evs = [tk[("b3prep", q * NCH + c0)], tk[("b3prep", q * NCH + c1)],
                   tk[("b2free", q * NCH + c0)], tk[("b2free", q * NCH + c1)]]
        mainA(c0, blk_evs)
        mainA(c1, blk_evs)
        phaseB(c0, True)
        phaseB(c1, False)

    lt = q * NCH + NCH - 1
    jt = P.op("sync", lambda E: E.nop(),
              deps=[tk[("s1d", lt)], tk.get(("s1d", lt - 1)), tk[("hbcast", lt)],
                    tk[("gate", lt)], tk.get(("gate", lt - 1)), tk[("ts1", lt)],
                    tk[("ypready", lt)], tk[("up", lt)], tk[("trdone", lt)],
                    tk[("hupd2", lt)], tk[("opdone", lt)]], inc="qpd")
    return jt


def pe1(B, stack):
    B.phase_bufs(stack, {
        "nx_xbuf": ((128, 2 * D), F32),
        "sq_scr": ((128, 2048), F32),
        "nx_f32": ((128, 2 * D), F32),
        "nx_st": ((128, 8), F32),
        "nx_tm": ((128, B.NTB * D), BF16),
        "nx_stage": ((128, 2 * B.T), BF16),
        "pe_g1": ((128, 8), F32),
    })
    g1 = B.din("w_g1", (128, 8), F32)
    B.P.dma("sync", B.sb["pe_g1"].ap(), g1, inc="dg1")
    B.P.barrier()
    nx_cm = B.dscratch("nx_cm", (D, B.T), BF16)
    emit_ln_to_cm(B, B.dram["xn1_tm"], nx_cm, "e1", kind="rms", eps=1e-6,
                  gain_col=B.sb["pe_g1"])


def pe2(B, stack):
    import contextlib
    P, T, NTB = B.P, B.T, B.NTB
    NH, DK = 4, 256
    NC = T // 512
    NS = T // 512
    nx_cm = B.dram["nx_cm"]
    xn1_tm = B.dram["xn1_tm"]
    X2_hbm = B.dscratch("X2_hbm", (T, D), F32)
    o_hbm = B.dscratch("o2_hbm", (NTB, NH, 128, DK), BF16)
    wqT = B.din("w_WqT", (D, D), F32)
    wkT = B.din("w_WkcT", (D, DK), F32)
    wvT = B.din("w_WvcT", (D, DK), F32)
    woT = B.din("w_WoT", (D, D), F32)

    psA, psB, psC = B.psA.ap(), B.psB.ap(), B.psC.ap()
    identf = B.sb["ident_f32"].ap()
    onesf = B.sb["ones_f32"].ap()

    gam = stack.enter_context(B.nc.sbuf_tensor("pe2_gam", [128, NH * NTB], F32))

    with contextlib.ExitStack() as s1:
        B.phase_bufs(s1, {
            "e2_xn": ((128, 8 * T), BF16),
            "e2_qT": ((128, 4 * T), BF16),
            "e2_kT": ((128, 2 * T), BF16),
            "e2_v": ((128, 2 * T), BF16),
            "e2_pT": ((128, 2 * T), BF16),
            "e2_scr": ((128, 2 * 1024), F32),
            "e2_st": ((128, 256), F32),
            "e2_wq": ((128, 8 * 256), BF16),
            "e2_wkv": ((128, 2 * 8 * 256), BF16),
            "e2_ostg": ((128, 2 * 512), BF16),
        })
        xn = B.sb["e2_xn"].ap().rearrange("p (a t) -> p a t", a=8)
        qT2 = B.sb["e2_qT"].ap()
        kT = B.sb["e2_kT"].ap()
        vv = B.sb["e2_v"].ap()
        pT = B.sb["e2_pT"].ap()
        scr2 = B.sb["e2_scr"].ap()
        st2 = B.sb["e2_st"].ap()
        wq = B.sb["e2_wq"].ap().rearrange("p (a d) -> p a d", a=8)
        wkv = B.sb["e2_wkv"].ap().rearrange("p (w a d) -> p w a d", w=2, a=8)
        ostg = B.sb["e2_ostg"].ap()

        tkv = None
        for wi, wdr in ((0, wkT), (1, wvT)):
            tkv = P.dma("gpsimd", wkv[:, wi, :, :], wdr.rearrange("(a p) d -> p a d", p=128),
                        deps=[], inc="dwkv")
        head_done = None
        kv_done = None
        prev_sd = {}
        for h in range(NH):
            tg = "e2h%d" % (h % 2)
            hb2 = h % 2
            qT = qT2[:, hb2 * 2 * T:(hb2 + 1) * 2 * T]
            st = st2[:, hb2 * 128:(hb2 + 1) * 128]
            scr = scr2[:, hb2 * 1024:(hb2 + 1) * 1024]
            prev2 = prev_sd.get(("head", h - 2), [])
            if h == 0:
                for mt in range(8):
                    xn_tok = P.dma("sync", xn[:, mt, :], nx_cm[mt * 128:(mt + 1) * 128, :],
                                   deps=[], inc="dxe0")
            w_tok = P.dma("gpsimd", wq, wqT[:, h * DK:(h + 1) * DK].rearrange("(a p) d -> p a d", p=128),
                          deps=[prev_sd.get("qkv_t")], inc="dwq2%d" % (h % 2))

            ev = {}
            for c in range(NC):
                for dkt in range(2):
                    psx = psA[:, dkt * 512:(dkt + 1) * 512]
                    mm = None
                    for mt in range(8):
                        dep = [xn_tok, w_tok, ev.get(("q", dkt, c - 1))]
                        if c == 0 and mt == 0:
                            dep += prev2
                        mm = P.op("tensor",
                                  (lambda o=psx,
                                   w=wq[:, mt, dkt * 128:(dkt + 1) * 128],
                                   x=xn[:, mt, c * 512:(c + 1) * 512],
                                   s0=(mt == 0), s1=(mt == 7):
                                   lambda E: E.matmul(o, w, x, start=s0, stop=s1))(),
                                  deps=dep, inc=("t" + tg) if mt == 7 else None)
                    ev[("q", dkt, c)] = P.op("vector",
                                             (lambda o=qT.rearrange("p (k t) -> p k t", k=2)[:, dkt, c * 512:(c + 1) * 512],
                                              s=psx:
                                              lambda E: E.tensor_copy(o, s))(),
                                             deps=[mm, ev.get(("q", dkt, c - 1))]
                                             + (prev2 if c == 0 else []), inc="v" + tg)
                if h == 0:
                    for wi, dstT in ((0, kT), (1, None)):
                        for dkt in range(2):
                            psx = psC[:, (dkt % 2) * 512:(dkt % 2) * 512 + 512]
                            mm = None
                            bank_dep = (ev.get((0, dkt, c)) if wi == 1
                                        else ev.get((1, dkt, c - 1)))
                            for mt in range(8):
                                mm = P.op("tensor",
                                          (lambda o=psx, w=wkv[:, wi, mt, dkt * 128:(dkt + 1) * 128],
                                           x=xn[:, mt, c * 512:(c + 1) * 512],
                                           s0=(mt == 0), s1=(mt == 7):
                                           lambda E: E.matmul(o, w, x, start=s0, stop=s1))(),
                                          deps=[xn_tok, tkv, bank_dep],
                                          inc=("t" + tg) if mt == 7 else None)
                            if wi == 0:
                                ev[(wi, dkt, c)] = P.op(
                                    "vector",
                                    (lambda o=kT.rearrange("p (k t) -> p k t", k=2)[:, dkt, c * 512:(c + 1) * 512],
                                     s=psx: lambda E: E.tensor_copy(o, s))(),
                                    deps=[mm, ev.get((wi, dkt, c - 1))], inc="v" + tg)
                            else:
                                ev[(wi, dkt, c)] = P.op(
                                    "vector",
                                    (lambda o=pT.rearrange("p (k t) -> p k t", k=2)[:, dkt, c * 512:(c + 1) * 512],
                                     s=psx: lambda E: E.tensor_copy(o, s))(),
                                    deps=[mm, ev.get((wi, dkt, c - 1))], inc="v" + tg)
            prev_sd["qkv_t"] = P.op("sync", lambda E: E.nop(),
                                    deps=[ev[("q", 1, NC - 1)]], inc="qe2j")
            if h == 0:
                pTv = pT.rearrange("p (k t) -> p k t", k=2)
                srcs, sdeps = [], []
                for tb in range(NTB):
                    for dkt in range(2):
                        srcs.append(pTv[:, dkt, tb * 128:(tb + 1) * 128])
                        sdeps.append([ev[(1, dkt, tb // 4)]])
                vt_done, _ = transpose_tiles(B, srcs, sdeps, vv, 0, tg)
                kv_done = vt_done
            negm_c = st[:, 0:8]
            Zc = st[:, 8:16]
            nm_q = st[:, 16:17]
            e_sc = st[:, 18:26]
            negMh = st[:, 26:27]
            Mh = st[:, 27:28]
            nma = st[:, 32:32 + NTB]
            za = st[:, 64:64 + NTB]
            lnZa = st[:, 96:96 + NTB]
            bank_tok = {}
            comb_v = {}
            comb_s = {}
            qk_dep = [ev[("q", 1, NC - 1)], kv_done, ev.get((0, 1, NC - 1))]
            for qb in range(NTB):
                for c in range(NC):
                    par = c % 2
                    psx = psA[:, par * 512:(par + 1) * 512]
                    mmx = None
                    for dkt in range(2):
                        mmx = P.op("tensor",
                                   (lambda o=psx,
                                    w=qT.rearrange("p (k t) -> p k t", k=2)[:, dkt, qb * 128:(qb + 1) * 128],
                                    x=kT.rearrange("p (k t) -> p k t", k=2)[:, dkt, c * 512:(c + 1) * 512],
                                    s0=(dkt == 0), s1=(dkt == 1):
                                    lambda E: E.matmul(o, w, x, start=s0, stop=s1))(),
                                   deps=qk_dep + [bank_tok.get(par)],
                                   inc=("t" + tg) if dkt == 1 else None)
                    t_mx = P.op("vector",
                                (lambda o=negm_c[:, c:c + 1], s=psx:
                                 lambda E: E.reduce_max(o, s, axis=AX.X, negate=True))(),
                                deps=[mmx, comb_v.get(qb - 1), comb_s.get(qb - 1)]
                                + (prev2 if qb == 0 else []), inc="v" + tg)
                    t_mx = P.op("vector",
                                (lambda o=negm_c[:, c:c + 1], s=negm_c[:, c:c + 1]:
                                 lambda E: E.tensor_scalar(o, s, 0.0625, None, OP.mult))(),
                                deps=[t_mx], inc="v" + tg)
                    tok = P.op("scalar",
                               (lambda o=scr[:, par * 512:(par + 1) * 512], s=psx,
                                b=negm_c[:, c:c + 1], z=Zc[:, c:c + 1]:
                                lambda E: E.activation(o, s, AF.Exp, bias=b, scale=0.0625,
                                                       accum_out=z))(),
                               deps=[t_mx] + (prev2 if qb == 0 else []), inc="s" + tg)
                    bank_tok[par] = tok
                dps = [bank_tok.get(0), bank_tok.get(1)]
                t1 = P.op("vector", (lambda o=nm_q, s=negm_c[:, 0:NC]:
                                     lambda E: E.tensor_reduce(o, s, AX.X, OP.min))(),
                          deps=dps, inc="v" + tg)
                t2 = P.op("scalar", (lambda o=e_sc[:, 0:NC], s=negm_c[:, 0:NC], b=nm_q:
                                     lambda E: E.activation(o, s, AF.Exp, scale=-1.0))(),
                          deps=[t1], inc="s" + tg)
                t3 = P.op("vector", (lambda o=e_sc[:, 0:NC], a=Zc[:, 0:NC]:
                                     lambda E: E.tensor_mul(o, o, a))(),
                          deps=[t2], inc="v" + tg)
                t4 = P.op("vector", (lambda o=za[:, qb:qb + 1], s=e_sc[:, 0:NC]:
                                     lambda E: E.reduce_sum(o, s, axis=AX.X))(),
                          deps=[t3], inc="v" + tg)
                t5 = P.op("vector", (lambda o=nma[:, qb:qb + 1], s=nm_q: lambda E: E.tensor_copy(o, s))(),
                          deps=[t1], inc="v" + tg)
                comb_v[qb] = t5
                comb_s[qb] = t2
            t1 = P.op("vector", (lambda o=negMh, s=nma: lambda E: E.tensor_reduce(o, s, AX.X, OP.min))(),
                      deps=[comb_v[NTB - 1], comb_s[NTB - 1]], inc="v" + tg)
            t2 = P.op("tensor", (lambda o=psB[0:1, 0:128], s=negMh:
                                 lambda E: E.transpose(o, s, identf))(),
                      deps=[t1], inc="t" + tg)
            t3 = P.op("vector", (lambda o=scr[0:1, 0:128], s=psB[0:1, 0:128]:
                                 lambda E: E.tensor_copy(o, s))(), deps=[t2], inc="v" + tg)
            t4 = P.op("vector", (lambda o=scr[0:1, 128:129], s=scr[0:1, 0:128]:
                                 lambda E: E.tensor_reduce(o, s, AX.X, OP.min))(),
                      deps=[t3], inc="v" + tg)
            t5 = P.op("tensor", (lambda o=psB[:, 128:129], w=onesf, x=scr[0:1, 128:129]:
                                 lambda E: E.matmul(o, w, x, start=True, stop=True))(),
                      deps=[t4], inc="t" + tg)
            t6 = P.op("vector", (lambda o=negMh, s=psB[:, 128:129]:
                                 lambda E: E.tensor_copy(o, s))(), deps=[t5, t2], inc="v" + tg)
            t7 = P.op("scalar", (lambda o=Mh, s=negMh: lambda E: E.mul(o, s, -1.0))(),
                      deps=[t6], inc="s" + tg)
            t8 = P.op("scalar", (lambda o=lnZa, s=za: lambda E: E.activation(o, s, AF.Ln))(),
                      deps=[comb_v[NTB - 1]], inc="s" + tg)
            tA = P.op("vector", (lambda o=nma, a=nma, sc=Mh, b=lnZa:
                                 lambda E: E.scalar_tensor_tensor(o, a, sc, b, OP.add, OP.subtract))(),
                      deps=[t7, t8], inc="v" + tg)
            gslice = gam.ap()[:, h * NTB:(h + 1) * NTB]
            tG = P.op("scalar", (lambda o=gslice, s=nma: lambda E: E.activation(o, s, AF.Exp))(),
                      deps=[tA], inc="s" + tg)

            o_mm_done = {}
            o_ev = {}
            exp_tok = {}
            pso_prev = prev_sd.get("s_last")
            poev = [prev_sd.get("oev0"), prev_sd.get("oev1")]
            tev = None
            NSLOT = 2 * T // 512
            omm_jb = {}
            for s in range(NS):
                for jb in range(NTB):
                    pslot = pT[:, (jb % NSLOT) * 512:(jb % NSLOT + 1) * 512]
                    par = jb % 2
                    psx = psB[:, par * 512:(par + 1) * 512]
                    mmx = None
                    for dkt in range(2):
                        mmx = P.op("tensor",
                                   (lambda o=psx,
                                    w=kT.rearrange("p (k t) -> p k t", k=2)[:, dkt, jb * 128:(jb + 1) * 128],
                                    x=qT.rearrange("p (k t) -> p k t", k=2)[:, dkt, s * 512:(s + 1) * 512],
                                    s0=(dkt == 0), s1=(dkt == 1):
                                    lambda E: E.matmul(o, w, x, start=s0, stop=s1))(),
                                   deps=qk_dep + [exp_tok.get((s, jb - 2)), t6,
                                                  o_mm_done.get(s - 1) if jb < 2 else None,
                                                  pso_prev if (s == 0 and jb < 2) else None],
                                   inc=("t" + tg) if dkt == 1 else None)
                    wdep = [mmx, t6, o_mm_done.get(s - 1),
                            omm_jb.get((s, jb - NSLOT)),
                            kv_done if (s == 0 and h == 0) else None]
                    exp_tok[(s, jb)] = P.op("scalar",
                                            (lambda o=pslot, ss=psx, b=negMh:
                                             lambda E: E.activation(o, ss, AF.Exp, bias=b, scale=0.0625))(),
                                            deps=wdep, inc="s" + tg)
                    for dkt in range(2):
                        o_mm_done[s] = P.op(
                            "tensor",
                            (lambda o=psC[:, dkt * 512:dkt * 512 + 512],
                             w=vv[:, jb * 256 + dkt * 128:jb * 256 + (dkt + 1) * 128],
                             x=pslot,
                             s0=(jb == 0), s1=(jb == NTB - 1):
                             lambda E: E.matmul(o, w, x, start=s0, stop=s1))(),
                            deps=[exp_tok[(s, jb)], kv_done, o_ev.get(s - 1),
                                  poev[dkt] if s == 0 else None],
                            inc=("t" + tg) if dkt == 1 else None)
                    omm_jb[(s, jb)] = o_mm_done[s]
                for dkt in range(2):
                    stg = ostg[:, dkt * 512:(dkt + 1) * 512]
                    tev = P.op("vector",
                               (lambda o=stg, s=psC[:, dkt * 512:dkt * 512 + 512]:
                                lambda E: E.tensor_copy(o, s))(),
                               deps=[o_mm_done[s], o_ev.get((s - 1, dkt)),
                                     poev[dkt] if s == 0 else None], inc="v" + tg)
                    dst = o_hbm[4 * s:4 * s + 4, h, :, dkt * 128:(dkt + 1) * 128].rearrange("b p t -> p b t")
                    o_ev[(s, dkt)] = P.dma("gpsimd", dst, stg.rearrange("p (b t) -> p b t", b=4),
                                           deps=[tev], inc="doe%d" % dkt)
                o_ev[s] = o_ev[(s, 1)]
            prev_sd["s_last"] = exp_tok[(NS - 1, NTB - 1)]
            prev_sd["oev0"] = o_ev[(NS - 1, 0)]
            prev_sd["oev1"] = o_ev[(NS - 1, 1)]
            prev_sd[("head", h)] = [o_mm_done[NS - 1], prev_sd["s_last"], tev, tG,
                                    comb_s[NTB - 1]]
        head_done = P.op("sync", lambda E: E.nop(),
                         deps=[o_ev[(NS - 1, 0)], o_ev[(NS - 1, 1)]], inc="qe2")
    P.barrier()

    with contextlib.ExitStack() as s2:
        B.phase_bufs(s2, {
            "e2_wo": ((128, 8 * D), BF16),
            "e2_ob": ((128, 2 * NH * DK), BF16),
            "e2_acc": ((128, 2 * D), F32),
            "e2_x1": ((128, 2 * D), F32),
        })
        wosb = B.sb["e2_wo"].ap().rearrange("p (c m) -> p c m", c=8)
        ob = B.sb["e2_ob"].ap().rearrange("p (b h t) -> p b h t", b=2, h=NH)
        acc = B.sb["e2_acc"].ap()
        x1b = B.sb["e2_x1"].ap().rearrange("p (b c) -> p b c", b=2)
        wo_tok = P.dma("gpsimd", wosb, woT.rearrange("(c p) m -> p c m", p=128), inc="dwo2")
        ld = {}
        lx = {}
        st_dma = {}
        accw = {}
        mmlast = {}
        bank_prev = {0: None, 1: None}
        for tb in range(NTB):
            buf = tb % 2
            ld[tb] = P.dma("sync", ob[:, buf, :, :],
                           B.dram["o2_hbm"][tb].rearrange("h p t -> p h t"),
                           deps=[accw.get(tb - 2), mmlast.get(tb - 2)], inc="dob2%d" % buf)
            lx[tb] = P.dma("sync", x1b[:, buf, :], xn1_tm[tb * 128:(tb + 1) * 128, :],
                           deps=[st_dma.get(tb - 2)], inc="dox%d" % buf)
            acc_a = acc[:, buf * D:(buf + 1) * D]
            for h in range(NH):
                ppar = h % 2
                psx = (psA if ppar == 0 else psB)
                mm2 = None
                for dkt in range(2):
                    ct = h * 2 + dkt
                    for cf in range(2):
                        mm2 = P.op("tensor",
                                   (lambda o=psx[:, cf * 512:(cf + 1) * 512],
                                    w=ob[:, buf, h, dkt * 128:(dkt + 1) * 128],
                                    x=wosb[:, ct, cf * 512:(cf + 1) * 512],
                                    s0=(dkt == 0), s1=(dkt == 1):
                                    lambda E: E.matmul(o, w, x, start=s0, stop=s1))(),
                                   deps=[ld[tb], wo_tok, bank_prev[ppar]],
                                   inc=("te2p" if (dkt == 1 and cf == 1) else None))
                gcol = gam.ap()[:, h * NTB + tb:h * NTB + tb + 1]
                if h == 0:
                    accw[(tb, h)] = P.op("vector",
                                         (lambda o=acc_a, s=psx, g=gcol:
                                          lambda E: E.tensor_scalar(o, s, g, None, OP.mult))(),
                                         deps=[mm2, st_dma.get(tb - 2)], inc="ve2p")
                else:
                    accw[(tb, h)] = P.op("vector",
                                         (lambda o=acc_a, s=psx, g=gcol, a=acc_a:
                                          lambda E: E.scalar_tensor_tensor(o, s, g, a, OP.mult, OP.add))(),
                                         deps=[mm2, accw[(tb, h - 1)]], inc="ve2p")
                bank_prev[ppar] = accw[(tb, h)]
            tx = P.op("vector", (lambda o=acc_a, a=acc_a, b=x1b[:, buf, :]:
                                 lambda E: E.tensor_add(o, a, b))(),
                      deps=[accw[(tb, NH - 1)], lx[tb]], inc="ve2p")
            accw[tb] = tx
            mmlast[tb] = mm2
            st_dma[tb] = P.dma("gpsimd", X2_hbm[tb * 128:(tb + 1) * 128, :], acc_a,
                               deps=[tx], inc="dsx%d" % buf)
    P.barrier()


def pe2b(B, stack):
    B.phase_bufs(stack, {
        "nx_xbuf": ((128, 2 * D), F32),
        "sq_scr": ((128, 2048), F32),
        "nx_f32": ((128, 2 * D), F32),
        "nx_st": ((128, 8), F32),
        "nx_tm": ((128, B.NTB * D), BF16),
        "nx_stage": ((128, 2 * B.T), BF16),
        "pe_g2": ((128, 8), F32),
    })
    g2 = B.din("w_g2", (128, 8), F32)
    B.P.dma("sync", B.sb["pe_g2"].ap(), g2, inc="dg2")
    B.P.barrier()
    nx2_cm = B.dscratch("nx2_cm", (D, B.T), BF16)
    emit_ln_to_cm(B, B.dram["X2_hbm"], nx2_cm, "e2b", kind="rms", eps=1e-6,
                  gain_col=B.sb["pe_g2"])


def pe3(B, stack):
    import contextlib
    P, T, NTB = B.P, B.T, B.NTB
    NE = 8
    GT = min(1024, T)
    NG = T // GT
    NTS = GT // 128
    FF = 1024
    NFT = FF // 128
    nx2_cm = B.dram["nx2_cm"]
    X2_hbm = B.dram["X2_hbm"]
    x1_hbm = B.dram["x1_hbm"]
    s1_hbm = B.dram["s1_hbm"]
    y = B.dout("y", (T, D), F32)
    wr = B.din("w_WrT", (D, NE), F32)
    eb = B.din("w_eb", (1, NE), F32)
    sw1 = B.din("w_sW1T", (D, FF), F32)
    sw3 = B.din("w_sW3T", (D, FF), F32)
    sw2 = B.din("w_sW2T", (FF, D), F32)
    ew1 = B.din("w_eW1T", (NE, D, FF), F32)
    ew3 = B.din("w_eW3T", (NE, D, FF), F32)
    ew2 = B.din("w_eW2T", (NE, FF, D), F32)

    psA, psB, psC = B.psA.ap(), B.psB.ap(), B.psC.ap()
    onesb = B.sb["ones_bf"].ap()

    w_sb = stack.enter_context(B.nc.sbuf_tensor("pe3_w", [128, NTB * 8], F32))

    with contextlib.ExitStack() as s1:
        B.phase_bufs(s1, {
            "e3_rxn": ((128, 2 * 8 * 128), BF16),
            "e3_wr": ((128, 8 * 8), BF16),
            "e3_row": ((1, 8), BF16),
            "e3_lg": ((128, 64), F32),
        })
        rxn = B.sb["e3_rxn"].ap().rearrange("p (b a t) -> p b a t", b=2, a=8)
        wrt = B.sb["e3_wr"].ap().rearrange("p (a e) -> p a e", a=8)
        ebr = B.sb["e3_row"].ap()
        lg = B.sb["e3_lg"].ap()
        tw = P.dma("gpsimd", wrt, wr.rearrange("(a p) e -> p a e", p=128), inc="dwr3")
        tw = P.dma("gpsimd", ebr, eb, deps=[], inc="dwr3")
        prev = {}
        ldx = {}
        for tb in range(NTB):
            buf = tb % 2
            ldx[tb] = P.dma("sync", rxn[:, buf, :, :],
                            nx2_cm.rearrange("(a p) t -> p a t", p=128)[:, :, tb * 128:(tb + 1) * 128],
                            deps=[prev.get(tb - 2)], inc="dxr%d" % buf)
            mmt = None
            for mt in range(8):
                mmt = P.op("tensor",
                           (lambda o=psA[:, 0:NE], w=rxn[:, buf, mt, :], x=wrt[:, mt, :],
                            s0=(mt == 0), s1=False:
                            lambda E: E.matmul(o, w, x, start=s0, stop=s1))(),
                           deps=[ldx[tb], tw, prev.get(tb - 1)], inc=None)
            mmt = P.op("tensor",
                       (lambda o=psA[:, 0:NE], w=onesb[0:1, :], x=ebr[0:1, 0:NE]:
                        lambda E: E.matmul(o, w, x, start=False, stop=True))(),
                       deps=[], inc="t3r")
            c0 = tb % 2
            gat = lg[:, c0 * 32:c0 * 32 + 8]
            ex = lg[:, c0 * 32 + 8:c0 * 32 + 16]
            m8 = lg[:, c0 * 32 + 16:c0 * 32 + 24]
            sc = lg[:, c0 * 32 + 24:c0 * 32 + 32]
            t1 = P.op("vector", (lambda o=sc[:, 0:1], s=psA[:, 0:NE]:
                                 lambda E: E.reduce_max(o, s, axis=AX.X, negate=True))(),
                      deps=[mmt, prev.get(tb - 2)], inc="v3r")
            t2 = P.op("scalar", (lambda o=ex, s=psA[:, 0:NE], b=sc[:, 0:1], z=sc[:, 1:2]:
                                 lambda E: E.activation(o, s, AF.Exp, bias=b, accum_out=z))(),
                      deps=[t1], inc="s3r")
            t3 = P.op("vector", (lambda o=sc[:, 2:3], s=sc[:, 1:2]: lambda E: E.reciprocal(o, s))(),
                      deps=[t2], inc="v3r")
            t4 = P.op("vector", (lambda o=gat, s=ex, r=sc[:, 2:3]:
                                 lambda E: E.tensor_scalar(o, s, r, None, OP.mult))(),
                      deps=[t3], inc="v3r")
            t5 = P.op("vector", (lambda o=m8, s=gat: lambda E: E.max(o, s))(),
                      deps=[t4], inc="v3r")
            t6 = P.op("scalar", (lambda o=sc[:, 3:4], s=m8[:, 0:1]:
                                 lambda E: E.activation(o, s, AF.Exp))(),
                      deps=[t5], inc="s3r")
            t7 = P.op("scalar", (lambda o=sc[:, 4:5], s=m8[:, 1:2]:
                                 lambda E: E.activation(o, s, AF.Exp))(),
                      deps=[t5], inc="s3r")
            t8 = P.op("vector", (lambda o=sc[:, 5:6], a=sc[:, 3:4], b=sc[:, 4:5]:
                                 lambda E: E.tensor_add(o, a, b))(),
                      deps=[t6, t7], inc="v3r")
            t9 = P.op("vector", (lambda o=sc[:, 5:6], s=sc[:, 5:6]: lambda E: E.reciprocal(o, s))(),
                      deps=[t8], inc="v3r")
            tA = P.op("vector", (lambda o=sc[:, 6:7], a=sc[:, 3:4], b=sc[:, 5:6]:
                                 lambda E: E.tensor_mul(o, a, b))(),
                      deps=[t9], inc="v3r")
            tB = P.op("vector", (lambda o=sc[:, 7:8], a=sc[:, 4:5], b=sc[:, 5:6]:
                                 lambda E: E.tensor_mul(o, a, b))(),
                      deps=[t9], inc="v3r")
            tC = P.op("vector", (lambda o=ex, s=gat, m=m8[:, 0:1]:
                                 lambda E: E.tensor_scalar(o, s, m, None, OP.is_equal))(),
                      deps=[tA, t4], inc="v3r")
            tD = P.op("vector", (lambda o=ex, s=ex, w=sc[:, 6:7]:
                                 lambda E: E.tensor_scalar(o, s, w, None, OP.mult))(),
                      deps=[tC], inc="v3r")
            tE = P.op("vector", (lambda o=m8, s=gat, m=m8[:, 1:2]:
                                 lambda E: E.tensor_scalar(o, s, m, None, OP.is_equal))(),
                      deps=[tD, tB], inc="v3r")
            tF = P.op("vector", (lambda o=w_sb.ap()[:, tb * 8:(tb + 1) * 8], s=m8,
                                 w=sc[:, 7:8], a=ex:
                                 lambda E: E.scalar_tensor_tensor(o, s, w, a, OP.mult, OP.add))(),
                      deps=[tE], inc="v3r")
            prev[tb] = tF
    P.barrier()

    with contextlib.ExitStack() as s2:
        B.phase_bufs(s2, {
            "e3_xn": ((128, 2 * 8 * GT), BF16),
            "e3_h": ((128, NFT * GT), BF16),
            "e3_h1s": ((128, 2 * 512), BF16),
            "e3_w1": ((128, 2 * 8 * 128), BF16),
            "e3_w3": ((128, 2 * 8 * 128), BF16),
            "e3_w2": ((128, 2 * NFT * D), BF16),
            "e3_acc": ((128, NTS * D), F32),
            "e3_xs": ((128, 2 * 3 * D), F32),
            "e3_g": ((128, 2 * D), F32),
        })
        xng = B.sb["e3_xn"].ap().rearrange("p (b a t) -> p b a t", b=2, a=8)
        hbuf = B.sb["e3_h"].ap().rearrange("p (i t) -> p i t", i=NFT)
        h1s = B.sb["e3_h1s"].ap()
        w1b = B.sb["e3_w1"].ap().rearrange("p (b a c) -> p b a c", b=2, a=8)
        w3b = B.sb["e3_w3"].ap().rearrange("p (b a c) -> p b a c", b=2, a=8)
        w2b = B.sb["e3_w2"].ap().rearrange("p (b i c) -> p b i c", b=2, i=NFT)
        acc = B.sb["e3_acc"].ap().rearrange("p (s c) -> p s c", s=NTS)
        xs = B.sb["e3_xs"].ap().rearrange("p (b k c) -> p b k c", b=2, k=3)
        gps = B.sb["e3_g"].ap().rearrange("p (b c) -> p b c", b=2)

        xn_ld = {}
        out_mm = {}
        psC_tok = {}
        pAB_tok = {}
        acc_free = {}
        acc_tok = {}
        ep_tok = {}
        st_dma = {}
        h_done = {}
        wtk = {}
        for g in range(NG):
            gb = g % 2
            xn_ld[g] = P.dma("sync", xng[:, gb, :, :],
                             nx2_cm.rearrange("(a p) t -> p a t", p=128)[:, :, g * GT:(g + 1) * GT],
                             deps=[out_mm.get((g - 2, 8))], inc="dx3e%d" % gb)
            for ei in range(9):
                e = ei - 1
                rb = ei % 2
                rk = (g, ei)
                rkm2 = (g, ei - 2) if ei >= 2 else (g - 1, 7 + ei)
                if ei == 0:
                    src1, src3, src2 = sw1, sw3, sw2
                else:
                    src1, src3, src2 = ew1[e], ew3[e], ew2[e]
                w2tok = P.dma("gpsimd", w2b[:, rb, :, :], src2.rearrange("(i p) c -> p i c", p=128),
                              deps=[out_mm.get(rkm2)], inc="dw23%d" % rb)
                hev = None
                for it in range(NFT):
                    sl = it % 2
                    git = (g * 9 + ei) * NFT + it
                    wtk[("w1", git)] = P.dma("gpsimd", w1b[:, sl, :, :],
                                             src1[:, it * 128:(it + 1) * 128].rearrange("(a p) c -> p a c", p=128),
                                             deps=[wtk.get(("mm1", git - 2)), out_mm.get(rkm2)],
                                             inc="dw13%d" % sl)
                    wtk[("w3", git)] = P.dma("gpsimd", w3b[:, sl, :, :],
                                             src3[:, it * 128:(it + 1) * 128].rearrange("(a p) c -> p a c", p=128),
                                             deps=[wtk.get(("mm3", git - 2)), out_mm.get(rkm2)],
                                             inc="dw33%d" % sl)
                    mm1h = {}
                    tslh = {}
                    for half in range(GT // 512):
                        pp = half % 2
                        ps1 = psC[:, pp * 512:pp * 512 + 512]
                        mm1 = None
                        for mt in range(8):
                            mm1 = P.op("tensor",
                                       (lambda o=ps1, w=w1b[:, sl, mt, :],
                                        xx=xng[:, gb, mt, half * 512:(half + 1) * 512],
                                        s0=(mt == 0), s1=(mt == 7):
                                        lambda E: E.matmul(o, w, xx, start=s0, stop=s1))(),
                                       deps=[xn_ld[g], wtk[("w1", git)], psC_tok.get(("g", pp))],
                                       inc=("t3e" if mt == 7 else None))
                        mm1h[half] = mm1
                        tslh[half] = P.op("scalar",
                                          (lambda o=h1s[:, pp * 512:pp * 512 + 512], s=ps1:
                                           lambda E: E.activation(o, s, AF.Silu))(),
                                          deps=[mm1, psC_tok.get(("g", pp))], inc="s3e")
                    wtk[("mm1", git)] = mm1h[GT // 512 - 1]
                    for half in range(GT // 512):
                        pp = half % 2
                        ps1 = psC[:, pp * 512:pp * 512 + 512]
                        mm3 = None
                        for mt in range(8):
                            mm3 = P.op("tensor",
                                       (lambda o=ps1, w=w3b[:, sl, mt, :],
                                        xx=xng[:, gb, mt, half * 512:(half + 1) * 512],
                                        s0=(mt == 0), s1=(mt == 7):
                                        lambda E: E.matmul(o, w, xx, start=s0, stop=s1))(),
                                       deps=[wtk[("w3", git)], tslh[half]],
                                       inc=("t3e" if mt == 7 else None))
                        wtk[("mm3", git)] = mm3
                        tmu = P.op("vector",
                                   (lambda o=hbuf[:, it, half * 512:(half + 1) * 512],
                                    a=h1s[:, pp * 512:pp * 512 + 512], s=ps1:
                                    lambda E: E.tensor_mul(o, a, s))(),
                                   deps=[mm3, out_mm.get((g, ei - 1) if ei >= 1 else (g - 1, 8))],
                                   inc="v3e")
                        psC_tok[("g", pp)] = tmu
                        hev = tmu
                h_done[rk] = hev
                mmt = None
                for ts in range(NTS):
                    pq = ts % 2
                    psx = (psA if pq == 0 else psB)
                    for it in range(NFT):
                        for cf in range(2):
                            mmt = P.op("tensor",
                                       (lambda o=psx[:, cf * 512:(cf + 1) * 512],
                                        w=hbuf[:, it, ts * 128:(ts + 1) * 128],
                                        xx=w2b[:, rb, it, cf * 512:(cf + 1) * 512],
                                        s0=(it == 0), s1=(it == NFT - 1):
                                        lambda E: E.matmul(o, w, xx, start=s0, stop=s1))(),
                                       deps=[hev, w2tok, pAB_tok.get(pq)],
                                       inc=("t3e" if (it == NFT - 1 and cf == 1) else None))
                    tbg = g * NTS + ts
                    wcol = w_sb.ap()[:, tbg * 8 + e:tbg * 8 + e + 1]
                    if ei == 0:
                        tok = P.op("vector",
                                   (lambda o=acc[:, ts, :], s=psx:
                                    lambda E: E.tensor_copy(o, s))(),
                                   deps=[mmt, acc_free.get((g - 1, ts))], inc="v3e")
                    else:
                        tok = P.op("vector",
                                   (lambda o=acc[:, ts, :], s=psx, w=wcol, a=acc[:, ts, :]:
                                    lambda E: E.scalar_tensor_tensor(o, s, w, a, OP.mult, OP.add))(),
                                   deps=[mmt, acc_tok.get(ts)], inc="v3e")
                    acc_tok[ts] = tok
                    pAB_tok[pq] = tok
                out_mm[rk] = mmt
            for ts in range(NTS):
                tbg = g * NTS + ts
                tsb = ts % 2
                pslot = (g, ts - 2) if ts >= 2 else (g - 1, NTS - 2 + ts)
                l2 = P.dma("sync", xs[:, tsb, 0, :], X2_hbm[tbg * 128:(tbg + 1) * 128, :],
                           deps=[ep_tok.get(("tX",) + pslot), st_dma.get(pslot)],
                           inc="dX%d" % tsb)
                l1_ = P.dma("sync", xs[:, tsb, 1, :], x1_hbm[tbg * 128:(tbg + 1) * 128, :],
                            deps=[ep_tok.get(("ty",) + pslot)], inc="dx1e%d" % tsb)
                ls = P.dma("sync", xs[:, tsb, 2, :], s1_hbm[tbg * 128:(tbg + 1) * 128, :],
                           deps=[ep_tok.get(("tm",) + pslot)], inc="ds1e%d" % tsb)
                tX = P.op("vector", (lambda o=gps[:, tsb, :], s=acc[:, ts, :], b=xs[:, tsb, 0, :]:
                                     lambda E: E.tensor_add(o, s, b))(),
                          deps=[acc_tok[ts], l2, ep_tok.get(("tg",) + pslot)], inc="v3e")
                ep_tok[("tX", g, ts)] = tX
                acc_free[(g, ts)] = tX
                tg_ = P.op("scalar", (lambda o=gps[:, tsb, :], s=gps[:, tsb, :]:
                                      lambda E: E.activation(o, s, AF.Silu))(),
                           deps=[tX], inc="s3e")
                ep_tok[("tg", g, ts)] = tg_
                tm = P.op("vector", (lambda o=gps[:, tsb, :], a=gps[:, tsb, :], b=xs[:, tsb, 2, :]:
                                     lambda E: E.tensor_mul(o, a, b))(),
                          deps=[tg_, ls], inc="v3e")
                ep_tok[("tm", g, ts)] = tm
                ty = P.op("vector", (lambda o=xs[:, tsb, 2, :], a=xs[:, tsb, 1, :], b=gps[:, tsb, :]:
                                     lambda E: E.tensor_add(o, a, b))(),
                          deps=[tm, l1_], inc="v3e")
                ep_tok[("ty", g, ts)] = ty
                st_dma[(g, ts)] = P.dma("gpsimd", y[tbg * 128:(tbg + 1) * 128, :],
                                        xs[:, tsb, 2, :], deps=[ty], inc="dy%d" % tsb)
        P.wait("gpsimd", [st_dma[(NG - 1, NTS - 1)], st_dma[(NG - 1, NTS - 2)]])
    P.barrier()


PHASE_FNS["p1"] = p1
PHASE_FNS["p2"] = p2
PHASE_FNS["p3"] = p3
PHASE_FNS["p4"] = p4
PHASE_FNS["pd"] = pd
PHASE_FNS["pe1"] = pe1
PHASE_FNS["pe2"] = pe2
PHASE_FNS["pe2b"] = pe2b
PHASE_FNS["pe3"] = pe3


ALL_PHASES = ("p1", "p2", "p3", "p4", "pd", "pe1", "pe2", "pe2b", "pe3")


def kernel(**inputs):
    T = 4096
    m, xs = host_prep(inputs, T)
    B = build_program(T, ALL_PHASES)
    cpool = np.zeros((128, 16), dtype=np.float32)
    for i, v in enumerate(B.constf_vals):
        cpool[:, i] = v
    m["c_pool_src"] = cpool
    declared = {k for k, v in B.dram_kind.items() if v == "in"}
    in_maps = [{k: v for k, v in dict(m, x=xs[b]).items() if k in declared}
               for b in range(8)]
    res = run_bass_kernel_spmd(B.nc, in_maps, core_ids=list(range(8)))
    return np.stack([res.results[b]["y"] for b in range(8)], axis=0)
